# version 01
# baseline (speedup 1.0000x reference)
import numpy as np

import concourse.bass as bass
import concourse.mybir as mybir
from concourse.bass_utils import run_bass_kernel_spmd

F32 = mybir.dt.float32
BF16 = mybir.dt.bfloat16
AF = mybir.ActivationFunctionType
ALU = mybir.AluOpType
AX = mybir.AxisListType

ENGINES = ("sync", "scalar", "vector", "tensor", "gpsimd")

T = 2048
D = 1024
NT = T // 128
ND = D // 128



class _Op:
    __slots__ = ("idx", "engine", "fn", "is_dma", "deps", "milestone",
                 "ms_no", "dma_no", "tag", "waits")

    def __init__(self, idx, engine, fn, is_dma, tag):
        self.idx = idx
        self.engine = engine
        self.fn = fn
        self.is_dma = is_dma
        self.deps = []
        self.milestone = False
        self.ms_no = None
        self.dma_no = None
        self.tag = tag
        self.waits = []


class Prog:
    def __init__(self, nc):
        self.nc = nc
        self.ops = []
        self.last_writer = {}
        self.readers = {}

    def op(self, engine, fn, reads=(), writes=(), tag=""):
        o = _Op(len(self.ops), engine, fn, False, tag)
        self._record(o, reads, writes)
        return o

    def dma(self, qengine, out, in_, reads=(), writes=(), transpose=False,
            tag=""):
        if transpose:
            fn = lambda eng: eng.dma_start_transpose(out=out, in_=in_)
        else:
            fn = lambda eng: eng.dma_start(out=out, in_=in_)
        o = _Op(len(self.ops), qengine, fn, True, tag)
        self._record(o, reads, writes)
        return o

    @staticmethod
    def _flatten(keys):
        out = []
        for k in keys:
            if isinstance(k, str):
                out.append(k)
            else:
                out.extend(k)
        return out

    def _record(self, o, reads, writes):
        reads = self._flatten(reads)
        writes = self._flatten(writes)
        deps = set()
        for r in reads:
            w = self.last_writer.get(r)
            if w is not None:
                deps.add(w)
        for w_ in writes:
            lw = self.last_writer.get(w_)
            if lw is not None:
                deps.add(lw)
            for rd in self.readers.get(w_, ()):
                deps.add(rd)
        for r in reads:
            self.readers.setdefault(r, []).append(o)
        for w_ in writes:
            self.last_writer[w_] = o
            self.readers[w_] = [o]
        deps.discard(o)
        o.deps = sorted(deps, key=lambda d: d.idx)
        self.ops.append(o)

    @staticmethod
    def _edge_needed(dep, o):
        if dep is o:
            return False
        if (dep.engine == "tensor" and o.engine == "tensor"
                and not dep.is_dma and not o.is_dma):
            return False
        if o.is_dma and not dep.is_dma and dep.engine == o.engine:
            return False
        return True

    def emit(self, comp_sems, dma_sems):
        for o in self.ops:
            best = {}
            for d in o.deps:
                if not self._edge_needed(d, o):
                    continue
                key = (d.engine, d.is_dma)
                if key not in best or best[key].idx < d.idx:
                    best[key] = d
            o.deps = sorted(best.values(), key=lambda d: d.idx)
            for d in o.deps:
                if not d.is_dma:
                    d.milestone = True
        ms_count = {e: 0 for e in ENGINES}
        dma_count = {e: 0 for e in ENGINES}
        for o in self.ops:
            if o.is_dma:
                dma_count[o.engine] += 1
                o.dma_no = dma_count[o.engine]
            elif o.milestone:
                ms_count[o.engine] += 1
                o.ms_no = ms_count[o.engine]
        for e in ENGINES:
            assert ms_count[e] < 60000, (e, ms_count[e])
            assert dma_count[e] * 16 < 60000, (e, dma_count[e])

        state = {e: {} for e in ENGINES}
        for o in self.ops:
            waits = {}
            for d in o.deps:
                if not self._edge_needed(d, o):
                    continue
                if d.is_dma:
                    key, val = ("dma", d.engine), 16 * d.dma_no
                else:
                    key, val = ("comp", d.engine), d.ms_no
                if waits.get(key, 0) < val:
                    waits[key] = val
            st = state[o.engine]
            for key, val in sorted(waits.items()):
                if st.get(key, 0) >= val:
                    continue
                st[key] = val
                o.waits.append((key, val))

        per_engine = {e: [o for o in self.ops if o.engine == e]
                      for e in ENGINES}
        counts = {"n": 0}

        def emit_stream(eng_name):
            def run(eng):
                for o in per_engine[eng_name]:
                    for (kind, src), val in o.waits:
                        sem = dma_sems[src] if kind == "dma" else comp_sems[src]
                        eng.wait_ge(sem, val)
                    inst = o.fn(eng)
                    if o.is_dma:
                        inst.then_inc(dma_sems[eng_name], 16)
                    elif o.milestone:
                        assert inst is not None, (o.tag, o.engine)
                        inst.then_inc(comp_sems[eng_name], 1)
                    counts["n"] += 1
                if eng_name == "sync":
                    for e2 in ENGINES:
                        if dma_count[e2]:
                            eng.wait_ge(dma_sems[e2], 16 * dma_count[e2])
                        if ms_count[e2] and e2 != "sync":
                            eng.wait_ge(comp_sems[e2], ms_count[e2])
            return run

        with self.nc.Block() as block:
            block.sync(emit_stream("sync"))
            block.scalar(emit_stream("scalar"))
            block.vector(emit_stream("vector"))
            block.tensor(emit_stream("tensor"))
            block.gpsimd(emit_stream("gpsimd"))
        return counts["n"], ms_count, dma_count



class Arena:
    def __init__(self, ap, n_words):
        self.ap = ap
        self.n = n_words
        self.off = 0
        self.marks = []
        self.peak = 0

    def _take(self, words, align=16):
        off = (self.off + align - 1) // align * align
        assert off + words <= self.n, \
            f"arena overflow: {off}+{words} > {self.n}"
        self.off = off + words
        self.peak = max(self.peak, self.off)
        return off

    PAGE = 256

    def _pages(self, off, words):
        return tuple(f"a{pg}" for pg in range(off // self.PAGE,
                                              (off + words - 1) // self.PAGE + 1))

    def f32(self, rows, cols, name):
        off = self._take(cols)
        return self.ap[0:rows, off:off + cols], self._pages(off, cols)

    def bf16(self, rows, cols, name):
        words = (cols + 1) // 2
        off = self._take(words)
        ap = self.ap[:, off:off + words].bitcast(BF16)[0:rows, 0:cols]
        return ap, self._pages(off, words)

    def mark(self):
        self.marks.append(self.off)

    def release(self):
        self.off = self.marks.pop()


class Banks:
    def __init__(self, pairs):
        self.pairs = pairs
        self.i = 0

    def take(self):
        ap, key = self.pairs[self.i % len(self.pairs)]
        self.i += 1
        return ap, key



class Builder:
    def __init__(self, nc, prog, arena, banks, hbm):
        self.nc = nc
        self.p = prog
        self.a = arena
        self.banks = banks
        self.hbm = hbm
        self.c = {}


    def prelude(self):
        a, p = self.a, self.p
        iota, iokey = a.f32(128, 128, "iota")
        p.op("gpsimd", lambda g: g.iota(iota, pattern=[[-1, 128]], base=0,
                                        channel_multiplier=1,
                                        allow_small_or_imprecise_dtypes=True),
             writes=[iokey], tag="iota")
        identf, ifkey = a.f32(128, 128, "identf")
        p.op("vector", lambda v: v.tensor_single_scalar(identf, iota, 0.0,
                                                        ALU.is_equal),
             reads=[iokey], writes=[ifkey], tag="identf")
        identb, ibkey = a.bf16(128, 128, "identb")
        p.op("vector", lambda v: v.tensor_copy(identb, identf),
             reads=[ifkey], writes=[ibkey], tag="identb")
        self.c["identf"] = (identf, ifkey)
        self.c["identb"] = (identb, ibkey)
        i1, i1key = a.f32(1, 1, "identf1")
        p.op("vector", lambda v: v.memset(i1, 1.0), writes=[i1key],
             tag="identf1")
        self.c["identf1"] = (i1, i1key)
        eps, ekey = a.f32(128, 1, "eps_col")
        p.op("vector", lambda v: v.memset(eps, 1e-5), writes=[ekey],
             tag="eps")
        self.c["eps_col"] = (eps, ekey)
        ones_bf, obkey = a.bf16(1, T, "ones_row_bf")
        p.op("vector", lambda v: v.memset(ones_bf, 1.0), writes=[obkey],
             tag="ones_row_bf")
        self.c["ones_row_bf"] = (ones_bf, obkey)
        ones_col_bf, ockey = a.bf16(128, 1, "ones_col_bf")
        p.op("vector", lambda v: v.memset(ones_col_bf, 1.0), writes=[ockey],
             tag="ones_col_bf")
        self.c["ones_col_bf"] = (ones_col_bf, ockey)
        junk, jkey = a.bf16(128, 1024, "junk")
        self.c["junk"] = (junk, jkey)


    def load_row(self, hbm_name, n, name, slot=None):
        src = self.hbm[hbm_name]
        if slot is not None:
            src = src[slot]
        row, key = self.a.f32(1, n, name)
        self.p.dma("sync", row, src[None, :], reads=[hbm_name + "@hbm"],
                   writes=[key], tag="load_row:" + name)
        return row, key

    def row_to_cols(self, row, rowkey, n, name):
        nch = (n + 127) // 128
        cols, ckey = self.a.f32(128, nch, name)
        identf1 = self.c["identf1"]
        for cix in range(nch):
            w = min(128, n - cix * 128)
            bank, bkey = self.banks.take()
            self.p.op(
                "tensor",
                lambda t, c=cix, b=bank, w=w: t.transpose(
                    b[0:w, 0:1], row[0:1, c * 128:c * 128 + w], identf1[0]),
                reads=[rowkey, identf1[1]], writes=[bkey],
                tag=f"r2c:{name}:{cix}")
            self.p.op(
                "vector",
                lambda v, c=cix, b=bank, w=w: v.tensor_copy(
                    cols[0:w, c:c + 1], b[0:w, 0:1]),
                reads=[bkey], writes=[ckey], tag=f"r2c_ev:{name}:{cix}")
        return cols, ckey

    def load_cols(self, hbm_name, n, slot=None):
        nch = (n + 127) // 128
        cols, ckey = self.a.f32(128, nch, hbm_name + "_cols")
        self.a.mark()
        row, rkey = self.a.f32(1, n, hbm_name + "_row")
        src = self.hbm[hbm_name]
        if slot is not None:
            src = src[slot]
        self.p.dma("sync", row, src[None, :],
                   reads=[hbm_name + "@hbm"], writes=[rkey],
                   tag="load_row:" + hbm_name)
        identf1 = self.c["identf1"]
        for cix in range(nch):
            w = min(128, n - cix * 128)
            bank, bkey = self.banks.take()
            self.p.op(
                "tensor",
                lambda t, c=cix, b=bank, w=w, row=row: t.transpose(
                    b[0:w, 0:1], row[0:1, c * 128:c * 128 + w], identf1[0]),
                reads=[rkey, identf1[1]], writes=[bkey],
                tag=f"r2c:{hbm_name}:{cix}")
            self.p.op(
                "vector",
                lambda v, c=cix, b=bank, w=w: v.tensor_copy(
                    cols[0:w, c:c + 1], b[0:w, 0:1]),
                reads=[bkey], writes=[ckey], tag=f"r2c_ev:{hbm_name}:{cix}")
        self.a.release()
        return cols, ckey

    def bcast_row(self, hbm_name, n, rows=128, scale=None):
        t, key = self.a.f32(rows, n, hbm_name + "_bc")
        self.p.dma("sync", t,
                   self.hbm[hbm_name][None, :].partition_broadcast(rows),
                   reads=[hbm_name + "@hbm"], writes=[key],
                   tag="bcast:" + hbm_name)
        return t, key

    def ln_tile(self, x_ap, xkey, out_ap, outkey, rows=128, width=D, tag=""):
        a, p = self.a, self.p
        a.mark()
        junk, jkey = self.c["junk"]
        s12, skey = a.f32(rows, 2, f"lns{tag}")
        cols, ckey = a.f32(rows, 4, f"lnc{tag}")
        p.op("vector", lambda v: v.tensor_scalar(
            junk[0:rows, 0:width], x_ap, 1.0, 0.0, ALU.mult, ALU.add,
            accum_out=s12[0:rows, 0:1]),
            reads=[xkey], writes=[skey, jkey], tag=f"ln_s1{tag}")
        p.op("scalar", lambda s: s.activation(
            junk[0:rows, 0:width], x_ap, AF.Square,
            accum_out=s12[0:rows, 1:2]),
            reads=[xkey], writes=[skey, jkey], tag=f"ln_s2{tag}")
        inv = 1.0 / width
        p.op("vector", lambda v: v.tensor_scalar_mul(
            cols[0:rows, 0:1], s12[0:rows, 0:1], inv),
            reads=[skey], writes=[ckey], tag=f"ln_mu{tag}")
        p.op("vector", lambda v: v.tensor_scalar_mul(
            cols[0:rows, 2:3], s12[0:rows, 1:2], inv),
            reads=[skey], writes=[ckey], tag=f"ln_t0{tag}")
        p.op("vector", lambda v: v.tensor_mul(
            cols[0:rows, 3:4], cols[0:rows, 0:1], cols[0:rows, 0:1]),
            reads=[ckey], writes=[ckey], tag=f"ln_t1{tag}")
        p.op("vector", lambda v: v.tensor_sub(
            cols[0:rows, 2:3], cols[0:rows, 2:3], cols[0:rows, 3:4]),
            reads=[ckey], writes=[ckey], tag=f"ln_var{tag}")
        p.op("scalar", lambda s: s.activation(
            cols[0:rows, 3:4], cols[0:rows, 2:3], AF.Sqrt,
            bias=self.c["eps_col"][0][0:rows, :]),
            reads=[ckey, self.c["eps_col"][1]], writes=[ckey],
            tag=f"ln_sqrt{tag}")
        p.op("vector", lambda v: v.reciprocal(
            cols[0:rows, 1:2], cols[0:rows, 3:4]),
            reads=[ckey], writes=[ckey], tag=f"ln_recip{tag}")
        p.op("vector", lambda v: v.tensor_mul(
            cols[0:rows, 3:4], cols[0:rows, 0:1], cols[0:rows, 1:2]),
            reads=[ckey], writes=[ckey], tag=f"ln_nb0{tag}")
        p.op("vector", lambda v: v.tensor_scalar_mul(
            cols[0:rows, 3:4], cols[0:rows, 3:4], -1.0),
            reads=[ckey], writes=[ckey], tag=f"ln_nb1{tag}")
        p.op("scalar", lambda s: s.activation(
            out_ap, x_ap, AF.Identity, bias=cols[0:rows, 3:4],
            scale=cols[0:rows, 1:2]),
            reads=[xkey, ckey], writes=[outkey], tag=f"ln_out{tag}")
        a.release()


    def l0_attn(self):
        nc, p, a, hbm = self.nc, self.p, self.a, self.hbm
        SCALE = 0.125
        a.mark()

        qpk, kpk, vtok, ofm = [], [], [], []
        for c in range(ND):
            qpk.append(a.bf16(128, T, f"qpk{c}"))
            kpk.append(a.bf16(128, T, f"kpk{c}"))
        for tix in range(NT):
            vtok.append(a.bf16(128, 1024, f"vtok{tix}"))
        for c in range(ND):
            ofm.append(a.bf16(128, T, f"ofm{c}"))
        bq_cols, bq_ckey = self.load_cols("L0_attn_b_qkv", 3072)

        a.mark()
        p.dma("gpsimd", hbm["x0bf"], hbm["L0_attn_x"],
              reads=["L0_attn_x@hbm"], writes=["x0bf@hbm"], tag="x0 cast")
        xhat = []
        for c in range(ND):
            t, key = a.bf16(128, T, f"xhat{c}")
            p.dma("sync", t, hbm["x0bf"][:, c * 128:(c + 1) * 128],
                  reads=["x0bf@hbm"], writes=[key], transpose=True,
                  tag=f"xhat{c}")
            xhat.append((t, key))
        wsec = []
        for i in range(ND):
            wsec.append(a.bf16(128, 1024, f"wqkv_sec{i}"))
        vstage = [a.bf16(128, 512, f"vstage{k}") for k in range(2)]

        def load_wq_section(sec):
            for i in range(ND):
                t, key = wsec[i]
                p.dma("gpsimd", t,
                      hbm["L0_attn_w_qkv"][i * 128:(i + 1) * 128,
                                           sec * 1024:(sec + 1) * 1024],
                      reads=["L0_attn_w_qkv@hbm"], writes=[key],
                      tag=f"wqkv s{sec} i{i}")

        def qkv_chunk(oc, dest, dest_scale):
            so = oc % 8
            for tq in range(4):
                bank, bkey = self.banks.take()
                for i in range(ND):
                    p.op("tensor", lambda t, i=i, so=so, tq=tq, b=bank:
                         t.matmul(b, wsec[i][0][:, so * 128:(so + 1) * 128],
                                  xhat[i][0][:, tq * 512:(tq + 1) * 512],
                                  start=(i == 0), stop=(i == ND - 1)),
                         reads=[wsec[i][1], xhat[i][1]], writes=[bkey],
                         tag=f"qkvmm oc{oc} tq{tq} i{i}")
                if dest == "v":
                    vs, vskey = vstage[tq % 2]
                    p.op("scalar", lambda s, oc=oc, b=bank, vs=vs:
                         s.activation(vs, b, AF.Identity,
                                      bias=bq_cols[:, oc:oc + 1]),
                         reads=[bkey, bq_ckey], writes=[vskey],
                         tag=f"qkvev v oc{oc} tq{tq}")
                    p.dma("gpsimd",
                          hbm["vfm"][(oc - 16) * 128:(oc - 15) * 128,
                                     tq * 512:(tq + 1) * 512],
                          vs, reads=[vskey], writes=["vfm@hbm"],
                          tag=f"vfm st oc{oc} tq{tq}")
                else:
                    ap, key = dest
                    p.op("scalar", lambda s, oc=oc, tq=tq, b=bank, ap=ap,
                         sc=dest_scale:
                         s.activation(ap[:, tq * 512:(tq + 1) * 512], b,
                                      AF.Identity,
                                      bias=bq_cols[:, oc:oc + 1], scale=sc),
                         reads=[bkey, bq_ckey], writes=[key],
                         tag=f"qkvev oc{oc} tq{tq}")

        load_wq_section(2)
        for oc in range(16, 24):
            qkv_chunk(oc, "v", 1.0)
        for tix in range(NT):
            p.dma("sync", vtok[tix][0],
                  hbm["vfm"][:, tix * 128:(tix + 1) * 128],
                  reads=["vfm@hbm"], writes=[vtok[tix][1]], transpose=True,
                  tag=f"vtok{tix}")
        load_wq_section(0)
        for oc in range(0, 8):
            qkv_chunk(oc, qpk[oc], SCALE)
        load_wq_section(1)
        for oc in range(8, 16):
            qkv_chunk(oc, kpk[oc - 8], 1.0)
        a.release()

        a.mark()
        mx, mxkey = a.f32(128, 16, "amax")
        for c in range(ND):
            p.op("vector", lambda v, c=c: v.reduce_max(
                mx[:, c:c + 1], qpk[c][0][:, :], axis=AX.X,
                apply_absolute_value=True),
                reads=[qpk[c][1]], writes=[mxkey], tag=f"maxq{c}")
            p.op("vector", lambda v, c=c: v.reduce_max(
                mx[:, 8 + c:9 + c], kpk[c][0][:, :], axis=AX.X,
                apply_absolute_value=True),
                reads=[kpk[c][1]], writes=[mxkey], tag=f"maxk{c}")
        mx2, mx2key = a.f32(128, 2, "amax2")
        p.op("vector", lambda v: v.reduce_max(mx2[:, 0:1], mx[:, 0:8],
                                              axis=AX.X),
             reads=[mxkey], writes=[mx2key], tag="maxq")
        p.op("vector", lambda v: v.reduce_max(mx2[:, 1:2], mx[:, 8:16],
                                              axis=AX.X),
             reads=[mxkey], writes=[mx2key], tag="maxk")
        bank, bkey = self.banks.take()
        p.op("tensor", lambda t, b=bank: t.transpose(
            b[0:2, 0:128], mx2, self.c["identf"][0]),
            reads=[mx2key, self.c["identf"][1]], writes=[bkey], tag="maxT")
        mr, mrkey = a.f32(2, 130, "amaxr")
        p.op("vector", lambda v, b=bank: v.tensor_copy(mr[0:2, 0:128],
                                                       b[0:2, 0:128]),
             reads=[bkey], writes=[mrkey], tag="maxT_ev")
        p.op("vector", lambda v: v.reduce_max(mr[0:2, 128:129],
                                              mr[0:2, 0:128], axis=AX.X),
             reads=[mrkey], writes=[mrkey], tag="maxr")
        negm3, negmkey = a.f32(1, 3, "negM")
        negm = negm3[0:1, 0:1]
        bank2, b2key = self.banks.take()
        p.op("tensor", lambda t, b=bank2: t.transpose(
            b[0:1, 0:2], mr[0:2, 128:129], self.c["identf"][0][0:2, 0:2]),
            reads=[mrkey, self.c["identf"][1]], writes=[b2key], tag="negmT")
        p.op("vector", lambda v, b=bank2: v.tensor_copy(
            negm3[0:1, 1:3], b[0:1, 0:2]),
            reads=[b2key], writes=[negmkey], tag="negmTev")
        p.op("vector", lambda v: v.tensor_mul(
            negm3[0:1, 0:1], negm3[0:1, 1:2], negm3[0:1, 2:3]),
            reads=[negmkey], writes=[negmkey], tag="negm0")
        p.op("vector", lambda v: v.tensor_scalar_mul(
            negm3[0:1, 0:1], negm3[0:1, 0:1], -64.0),
            reads=[negmkey], writes=[negmkey], tag="negm1")

        qw, qwkey = a.bf16(128, T, "qwork")
        kw, kwkey = a.bf16(128, T, "kwork")
        p.op("vector", lambda v: v.memset(qw[64:128, :], 0.0),
             writes=[qwkey], tag="qw z")
        p.op("vector", lambda v: v.memset(kw[64:128, :], 0.0),
             writes=[kwkey], tag="kw z")
        p.op("scalar", lambda s: s.activation(
            qw[64:65, 0:T], self.c["ones_row_bf"][0], AF.Identity,
            scale=negm),
            reads=[self.c["ones_row_bf"][1], negmkey], writes=[qwkey],
            tag="qw stab")
        p.op("vector", lambda v: v.memset(kw[64:65, :], 1.0),
             writes=[kwkey], tag="kw stab")

        attnT = [a.bf16(128, 512, f"attnT{j}") for j in range(NT)]
        v65 = [a.bf16(128, 65, f"v65_{j}") for j in range(NT)]
        for j in range(NT):
            p.op("vector", lambda v, j=j: v.memset(v65[j][0][:, 64:65], 1.0),
                 writes=[v65[j][1]], tag=f"v65 ones{j}")
        srow, srkey = a.f32(1, T, "srow")
        rbc, rbckey = a.f32(64, T, "rbc")
        ou, oukey = a.f32(64, T, "ou")

        for h in range(16):
            ch, half = h // 2, h % 2
            p.op("vector", lambda v, ch=ch, half=half: v.tensor_copy(
                qw[0:64, :], qpk[ch][0][half * 64:half * 64 + 64, :]),
                reads=[qpk[ch][1]], writes=[qwkey], tag=f"qw cp h{h}")
            p.op("vector", lambda v, ch=ch, half=half: v.tensor_copy(
                kw[0:64, :], kpk[ch][0][half * 64:half * 64 + 64, :]),
                reads=[kpk[ch][1]], writes=[kwkey], tag=f"kw cp h{h}")
            for j in range(NT):
                p.op("vector", lambda v, j=j, h=h: v.tensor_copy(
                    v65[j][0][:, 0:64],
                    vtok[j][0][:, h * 64:(h + 1) * 64]),
                    reads=[vtok[j][1]], writes=[v65[j][1]],
                    tag=f"v65 cp h{h} j{j}")
            for ib in range(4):
                for j in range(NT):
                    bS, bSkey = self.banks.take()
                    p.op("tensor", lambda t, j=j, ib=ib, b=bS: t.matmul(
                        b, kw[:, j * 128:(j + 1) * 128],
                        qw[:, ib * 512:(ib + 1) * 512],
                        start=True, stop=True),
                        reads=[qwkey, kwkey], writes=[bSkey],
                        tag=f"qk h{h} ib{ib} j{j}")
                    p.op("scalar", lambda s, j=j, b=bS: s.activation(
                        attnT[j][0], b, AF.Exp),
                        reads=[bSkey], writes=[attnT[j][1]],
                        tag=f"exp h{h} ib{ib} j{j}")
                bO, bOkey = self.banks.take()
                for j in range(NT):
                    p.op("tensor", lambda t, j=j, b=bO: t.matmul(
                        b[0:65, 0:512], v65[j][0], attnT[j][0],
                        start=(j == 0), stop=(j == NT - 1)),
                        reads=[v65[j][1], attnT[j][1]], writes=[bOkey],
                        tag=f"av h{h} ib{ib} j{j}")
                p.op("scalar", lambda s, ib=ib, b=bO: s.copy(
                    srow[0:1, ib * 512:(ib + 1) * 512], b[64:65, 0:512]),
                    reads=[bOkey], writes=[srkey], tag=f"srow h{h} ib{ib}")
                p.op("vector", lambda v, ib=ib, b=bO: v.tensor_copy(
                    ou[0:64, ib * 512:(ib + 1) * 512], b[0:64, 0:512]),
                    reads=[bOkey], writes=[oukey], tag=f"ou h{h} ib{ib}")
            p.op("vector", lambda v: v.reciprocal(srow, srow),
                 reads=[srkey], writes=[srkey], tag=f"rec h{h}")
            p.dma("gpsimd", hbm["rrow"][None, :], srow, reads=[srkey],
                  writes=["rrow@hbm"], tag=f"rrow st h{h}")
            p.dma("sync", rbc,
                  hbm["rrow"][None, :].partition_broadcast(64),
                  reads=["rrow@hbm"], writes=[rbckey], tag=f"rbc h{h}")
            p.op("vector", lambda v, h=h, half=half: v.tensor_tensor(
                ofm[h // 2][0][half * 64:half * 64 + 64, :],
                ou[0:64, :], rbc[0:64, :], ALU.mult),
                reads=[oukey, rbckey], writes=[ofm[h // 2][1]],
                tag=f"onorm h{h}")
        a.release()

        a.mark()
        wproj = []
        for cix in range(ND):
            t, key = a.bf16(128, 1024, f"wproj{cix}")
            p.dma("gpsimd", t,
                  hbm["L0_attn_w_proj"][cix * 128:(cix + 1) * 128, :],
                  reads=["L0_attn_w_proj@hbm"], writes=[key],
                  tag=f"wproj{cix}")
            wproj.append((t, key))
        bproj_bc, bpkey = self.bcast_row("L0_attn_b_proj", D)
        x0t = [a.f32(128, D, f"x0t{k}") for k in range(2)]
        xpre = [a.f32(128, D, f"xpre{k}") for k in range(2)]
        x1t = [a.f32(128, D, f"x1t{k}") for k in range(2)]
        for tc in range(NT):
            x0, x0key = x0t[tc % 2]
            xp, xpkey = xpre[tc % 2]
            x1, x1key = x1t[tc % 2]
            p.dma("sync", x0,
                  hbm["L0_attn_x"][tc * 128:(tc + 1) * 128, :],
                  reads=["L0_attn_x@hbm"], writes=[x0key], tag=f"x0 ld {tc}")
            for ob in range(2):
                bank, bkey = self.banks.take()
                for cix in range(ND):
                    p.op("tensor", lambda t, cix=cix, tc=tc, ob=ob, b=bank:
                         t.matmul(b, ofm[cix][0][:, tc * 128:(tc + 1) * 128],
                                  wproj[cix][0][:, ob * 512:(ob + 1) * 512],
                                  start=(cix == 0), stop=(cix == ND - 1)),
                         reads=[ofm[cix][1], wproj[cix][1]], writes=[bkey],
                         tag=f"proj tc{tc} ob{ob} c{cix}")
                p.op("vector", lambda v, ob=ob, b=bank, xp=xp: v.tensor_tensor(
                    xp[:, ob * 512:(ob + 1) * 512], b,
                    bproj_bc[:, ob * 512:(ob + 1) * 512], ALU.add),
                    reads=[bkey, bpkey], writes=[xpkey],
                    tag=f"projev tc{tc} ob{ob}")
            p.op("vector", lambda v, xp=xp, x0=x0: v.tensor_add(xp, xp, x0),
                 reads=[xpkey, x0key], writes=[xpkey], tag=f"res tc{tc}")
            self.ln_tile(xp, xpkey, x1, x1key, tag=f"l0a{tc}")
            p.dma("sync", hbm["x1"][tc * 128:(tc + 1) * 128, :], x1,
                  reads=[x1key], writes=["x1@hbm"], tag=f"x1 st {tc}")
            p.dma("gpsimd", hbm["x1bf"][tc * 128:(tc + 1) * 128, :], x1,
                  reads=[x1key], writes=["x1bf@hbm"], tag=f"x1bf st {tc}")
        a.release()
        a.release()



    def _prep_weight(self, src, dst, O, I, scols):
        p, a, hbm = self.p, self.a, self.hbm
        f32b = [a.f32(128, I, f"wprep_f{k}") for k in range(2)]
        bfb = [a.bf16(128, I, f"wprep_b{k}") for k in range(2)]
        sc, sckey = scols
        for oc in range(O // 128):
            fap, fkey = f32b[oc % 2]
            bap, bkey = bfb[oc % 2]
            p.dma("sync", fap, hbm[src][oc * 128:(oc + 1) * 128, :],
                  reads=[src + "@hbm"], writes=[fkey], tag=f"wp ld {src} {oc}")
            p.op("vector", lambda v, fap=fap, bap=bap, oc=oc:
                 v.tensor_scalar_mul(bap, fap, sc[:, oc:oc + 1]),
                 reads=[fkey, sckey], writes=[bkey], tag=f"wp sc {src} {oc}")
            p.dma("gpsimd", hbm[dst][oc * 128:(oc + 1) * 128, :], bap,
                  reads=[bkey], writes=[dst + "@hbm"], tag=f"wp st {src} {oc}")

    def l0_ffn(self):
        p, a, hbm = self.p, self.a, self.hbm
        H = 4096
        a.mark()
        s1c = self.load_cols("L0_ffn_w1_s", H)
        s2c = self.load_cols("L0_ffn_w2_s", H)
        h1c = self.load_cols("L0_ffn_h1_s", H)
        h2c = self.load_cols("L0_ffn_h2_s", H)
        w3c = self.load_cols("L0_ffn_w3_s", D)

        a.mark()
        self._prep_weight("L0_ffn_w1_w", "w1bf", H, D, s1c)
        self._prep_weight("L0_ffn_w2_w", "w2bf", H, D, s2c)
        self._prep_weight("L0_ffn_w3_w", "w3bf", D, H, w3c)
        self._prep_weight("L0_ffn_h1_w", "h1bf", H, H, h1c)
        self._prep_weight("L0_ffn_h2_w", "h2bf", H, H, h2c)
        a.release()

        for ts in range(2):
            r0 = ts * 1024

            a.mark()
            xh = []
            for c in range(ND):
                t, key = a.bf16(128, 1024, f"xh{c}")
                p.dma("sync", t, hbm["x1bf"][r0:r0 + 1024,
                                             c * 128:(c + 1) * 128],
                      reads=["x1bf@hbm"], writes=[key], transpose=True,
                      tag=f"xh{c} ts{ts}")
                xh.append((t, key))
            w1w = [a.bf16(128, 8 * 512, f"w1w{k}") for k in range(2)]
            w2w = [a.bf16(128, 8 * 512, f"w2w{k}") for k in range(2)]
            hsil = [a.bf16(128, 512, f"hsil{k}") for k in range(2)]
            hst = [a.bf16(128, 512, f"hst{k}") for k in range(2)]

            def s1_load(ob):
                par = ob % 2
                for i in range(ND):
                    for (win, img) in ((w1w, "w1bf"), (w2w, "w2bf")):
                        t, key = win[par]
                        p.dma("sync", t[:, i * 512:(i + 1) * 512],
                              hbm[img][ob * 512:(ob + 1) * 512,
                                       i * 128:(i + 1) * 128],
                              reads=[img + "@hbm"], writes=[key],
                              transpose=True, tag=f"{img}w ob{ob} i{i}")

            def s1_compute(ob):
                par = ob % 2
                for tc in range(8):
                    bA, bAkey = self.banks.take()
                    bB, bBkey = self.banks.take()
                    for i in range(ND):
                        p.op("tensor", lambda t, i=i, tc=tc, par=par, b=bA:
                             t.matmul(b, xh[i][0][:, tc * 128:(tc + 1) * 128],
                                      w1w[par][0][:, i * 512:(i + 1) * 512],
                                      start=(i == 0), stop=(i == ND - 1)),
                             reads=[xh[i][1], w1w[par][1]], writes=[bAkey],
                             tag=f"s1A ts{ts} ob{ob} tc{tc} i{i}")
                        p.op("tensor", lambda t, i=i, tc=tc, par=par, b=bB:
                             t.matmul(b, xh[i][0][:, tc * 128:(tc + 1) * 128],
                                      w2w[par][0][:, i * 512:(i + 1) * 512],
                                      start=(i == 0), stop=(i == ND - 1)),
                             reads=[xh[i][1], w2w[par][1]], writes=[bBkey],
                             tag=f"s1B ts{ts} ob{ob} tc{tc} i{i}")
                    sil, silkey = hsil[tc % 2]
                    hs, hskey = hst[tc % 2]
                    p.op("scalar", lambda s, b=bA, sil=sil:
                         s.activation(sil, b, AF.Silu),
                         reads=[bAkey], writes=[silkey],
                         tag=f"s1 sil ts{ts} ob{ob} tc{tc}")
                    p.op("vector", lambda v, b=bB, sil=sil, hs=hs:
                         v.tensor_tensor(hs, b, sil, ALU.mult),
                         reads=[bBkey, silkey], writes=[hskey],
                         tag=f"s1 mul ts{ts} ob{ob} tc{tc}")
                    p.dma("gpsimd",
                          hbm["h"][r0 + tc * 128:r0 + (tc + 1) * 128,
                                   ob * 512:(ob + 1) * 512], hs,
                          reads=[hskey], writes=["h@hbm"],
                          tag=f"h st ts{ts} ob{ob} tc{tc}")

            s1_load(0)
            for ob in range(8):
                if ob + 1 < 8:
                    s1_load(ob + 1)
                s1_compute(ob)
            a.release()

            for (src_t, img, dst) in (("h", "h1bf", "g1"),
                                      ("g1", "h2bf", "g2")):
                a.mark()
                hf = []
                for ic in range(32):
                    t, key = a.bf16(128, 1024, f"hf{ic}")
                    p.dma("sync", t, hbm[src_t][r0:r0 + 1024,
                                                ic * 128:(ic + 1) * 128],
                          reads=[src_t + "@hbm"], writes=[key], transpose=True,
                          tag=f"{src_t}f{ic} ts{ts}")
                    hf.append((t, key))
                win = [a.bf16(128, 32 * 512, f"hw{k}") for k in range(2)]
                gst = [a.bf16(128, 512, f"gst{k}") for k in range(2)]

                def h_load(ob):
                    t, key = win[ob % 2]
                    for ic in range(32):
                        p.dma("sync", t[:, ic * 512:(ic + 1) * 512],
                              hbm[img][ob * 512:(ob + 1) * 512,
                                       ic * 128:(ic + 1) * 128],
                              reads=[img + "@hbm"], writes=[key],
                              transpose=True, tag=f"{img}w ob{ob} ic{ic}")

                def h_compute(ob):
                    t, key = win[ob % 2]
                    for tc in range(8):
                        bA, bAkey = self.banks.take()
                        for ic in range(32):
                            p.op("tensor",
                                 lambda tt, ic=ic, tc=tc, t=t, b=bA:
                                 tt.matmul(
                                     b, hf[ic][0][:, tc * 128:(tc + 1) * 128],
                                     t[:, ic * 512:(ic + 1) * 512],
                                     start=(ic == 0), stop=(ic == 31)),
                                 reads=[hf[ic][1], key], writes=[bAkey],
                                 tag=f"{img} ts{ts} ob{ob} tc{tc} ic{ic}")
                        g, gkey = gst[tc % 2]
                        p.op("scalar", lambda s, b=bA, g=g:
                             s.activation(g, b, AF.Gelu),
                             reads=[bAkey], writes=[gkey],
                             tag=f"{img} g ts{ts} ob{ob} tc{tc}")
                        p.dma("gpsimd",
                              hbm[dst][r0 + tc * 128:r0 + (tc + 1) * 128,
                                       ob * 512:(ob + 1) * 512], g,
                              reads=[gkey], writes=[dst + "@hbm"],
                              tag=f"{dst} st ts{ts} ob{ob} tc{tc}")

                h_load(0)
                for ob in range(8):
                    if ob + 1 < 8:
                        h_load(ob + 1)
                    h_compute(ob)
                a.release()

            a.mark()
            gf = []
            for ic in range(32):
                t, key = a.bf16(128, 1024, f"gf{ic}")
                p.dma("sync", t, hbm["g2"][r0:r0 + 1024,
                                           ic * 128:(ic + 1) * 128],
                      reads=["g2@hbm"], writes=[key], transpose=True,
                      tag=f"g2f{ic} ts{ts}")
                gf.append((t, key))
            w3w, w3wkey = a.bf16(128, 32 * 1024, "w3w")
            for ic in range(32):
                p.dma("sync", w3w[:, ic * 1024:(ic + 1) * 1024],
                      hbm["w3bf"][0:1024, ic * 128:(ic + 1) * 128],
                      reads=["w3bf@hbm"], writes=[w3wkey], transpose=True,
                      tag=f"w3w{ic} ts{ts}")
            x1t = [a.f32(128, D, f"fx1{k}") for k in range(2)]
            xpre = [a.f32(128, D, f"fxp{k}") for k in range(2)]
            x2t = [a.f32(128, D, f"fx2{k}") for k in range(2)]
            for tc in range(8):
                x1, x1key = x1t[tc % 2]
                xp, xpkey = xpre[tc % 2]
                x2, x2key = x2t[tc % 2]
                p.dma("sync", x1, hbm["x1"][r0 + tc * 128:r0 + (tc + 1) * 128, :],
                      reads=["x1@hbm"], writes=[x1key], tag=f"x1 ld {ts}{tc}")
                bA, bAkey = self.banks.take()
                bB, bBkey = self.banks.take()
                for ic in range(32):
                    p.op("tensor", lambda t, ic=ic, tc=tc, b=bA:
                         t.matmul(b, gf[ic][0][:, tc * 128:(tc + 1) * 128],
                                  w3w[:, ic * 1024:ic * 1024 + 512],
                                  start=(ic == 0), stop=(ic == 31)),
                         reads=[gf[ic][1], w3wkey], writes=[bAkey],
                         tag=f"w3A ts{ts} tc{tc} ic{ic}")
                    p.op("tensor", lambda t, ic=ic, tc=tc, b=bB:
                         t.matmul(b, gf[ic][0][:, tc * 128:(tc + 1) * 128],
                                  w3w[:, ic * 1024 + 512:(ic + 1) * 1024],
                                  start=(ic == 0), stop=(ic == 31)),
                         reads=[gf[ic][1], w3wkey], writes=[bBkey],
                         tag=f"w3B ts{ts} tc{tc} ic{ic}")
                p.op("vector", lambda v, b=bA, xp=xp, x1=x1:
                     v.tensor_tensor(xp[:, 0:512], b, x1[:, 0:512], ALU.add),
                     reads=[bAkey, x1key], writes=[xpkey],
                     tag=f"w3evA ts{ts} tc{tc}")
                p.op("vector", lambda v, b=bB, xp=xp, x1=x1:
                     v.tensor_tensor(xp[:, 512:1024], b, x1[:, 512:1024],
                                     ALU.add),
                     reads=[bBkey, x1key], writes=[xpkey],
                     tag=f"w3evB ts{ts} tc{tc}")
                self.ln_tile(xp, xpkey, x2, x2key, tag=f"l0f{ts}{tc}")
                p.dma("sync", hbm["x2"][r0 + tc * 128:r0 + (tc + 1) * 128, :],
                      x2, reads=[x2key], writes=["x2@hbm"],
                      tag=f"x2 st {ts}{tc}")
                p.dma("gpsimd", hbm["x2bf"][r0 + tc * 128:r0 + (tc + 1) * 128, :],
                      x2, reads=[x2key], writes=["x2bf@hbm"],
                      tag=f"x2bf st {ts}{tc}")
            a.release()
        a.release()


    def l1_attn(self):
        p, a, hbm = self.p, self.a, self.hbm
        a.mark()

        vtok = [a.bf16(128, 2048, f"avtok{t}") for t in range(NT)]
        qt, qtkey = a.bf16(128, T, "aqt")
        kt, ktkey = a.bf16(128, T, "akt")

        a.mark()
        g_bc, gbkey = self.bcast_row("L1_attn_ln_scale", D)
        b_bc, bbkey = self.bcast_row("L1_attn_ln_bias", D)
        x2t = [a.f32(128, D, f"ax2{k}") for k in range(2)]
        xnt = [a.f32(128, D, f"axn{k}") for k in range(2)]
        for tc in range(NT):
            x2, x2key = x2t[tc % 2]
            xn, xnkey = xnt[tc % 2]
            p.dma("sync", x2, hbm["x2"][tc * 128:(tc + 1) * 128, :],
                  reads=["x2@hbm"], writes=[x2key], tag=f"ax2 ld {tc}")
            self.ln_tile(x2, x2key, xn, xnkey, tag=f"l1a{tc}")
            p.op("vector", lambda v, xn=xn: v.tensor_mul(xn, xn, g_bc),
                 reads=[xnkey, gbkey], writes=[xnkey], tag=f"xng {tc}")
            p.op("vector", lambda v, xn=xn: v.tensor_add(xn, xn, b_bc),
                 reads=[xnkey, bbkey], writes=[xnkey], tag=f"xnb {tc}")
            p.dma("gpsimd", hbm["xnbf"][tc * 128:(tc + 1) * 128, :], xn,
                  reads=[xnkey], writes=["xnbf@hbm"], tag=f"xnbf st {tc}")
        a.release()

        a.mark()
        xnf = []
        for c in range(ND):
            t, key = a.bf16(128, T, f"axnf{c}")
            p.dma("sync", t, hbm["xnbf"][:, c * 128:(c + 1) * 128],
                  reads=["xnbf@hbm"], writes=[key], transpose=True,
                  tag=f"axnf{c}")
            xnf.append((t, key))
        bh_bc, bhkey = a.f32(128, 2048, "bh_bc")
        whid = [a.bf16(128, 2048, f"awh{d}") for d in range(ND)]
        hstg = [a.f32(128, 512, f"ahstg{k}") for k in range(2)]
        gstg = [a.bf16(128, 512, f"agstg{k}") for k in range(2)]

        for half in range(2):
            p.dma("sync", bh_bc,
                  self.hbm["L1_attn_b_hidden"][half * 2048:(half + 1) * 2048]
                  [None, :].partition_broadcast(128),
                  reads=["L1_attn_b_hidden@hbm"], writes=[bhkey],
                  tag=f"bh_bc h{half}")
            for d in range(ND):
                t, key = whid[d]
                p.dma("gpsimd", t,
                      hbm["L1_attn_W_hidden"][d * 128:(d + 1) * 128,
                                              half * 2048:(half + 1) * 2048],
                      reads=["L1_attn_W_hidden@hbm"], writes=[key],
                      tag=f"awh{d} h{half}")
            for tc in range(NT):
                for ob in range(4):
                    gob = half * 4 + ob
                    bank, bkey = self.banks.take()
                    for d in range(ND):
                        p.op("tensor", lambda t, d=d, tc=tc, ob=ob, b=bank:
                             t.matmul(b, xnf[d][0][:, tc * 128:(tc + 1) * 128],
                                      whid[d][0][:, ob * 512:(ob + 1) * 512],
                                      start=(d == 0), stop=(d == ND - 1)),
                             reads=[xnf[d][1], whid[d][1]], writes=[bkey],
                             tag=f"hid tc{tc} ob{gob} d{d}")
                    hs, hskey = hstg[tc % 2]
                    p.op("vector", lambda v, b=bank, hs=hs, ob=ob:
                         v.tensor_tensor(hs, b,
                                         bh_bc[:, ob * 512:(ob + 1) * 512],
                                         ALU.add),
                         reads=[bkey, bhkey], writes=[hskey],
                         tag=f"hidb tc{tc} ob{ob}")
                    if half == 0:
                        p.op("scalar", lambda s, hs=hs, tc=tc, ob=ob:
                             s.activation(vtok[tc][0][:, ob * 512:
                                                      (ob + 1) * 512],
                                          hs, AF.Silu),
                             reads=[hskey], writes=[vtok[tc][1]],
                             tag=f"hidv tc{tc} ob{ob}")
                    else:
                        gs, gskey = gstg[tc % 2]
                        p.op("scalar", lambda s, hs=hs, gs=gs:
                             s.activation(gs, hs, AF.Silu),
                             reads=[hskey], writes=[gskey],
                             tag=f"hidg tc{tc} ob{ob}")
                        p.dma("gpsimd",
                              hbm["gate"][tc * 128:(tc + 1) * 128,
                                          ob * 512:(ob + 1) * 512], gs,
                              reads=[gskey], writes=["gate@hbm"],
                              tag=f"gate st tc{tc} ob{ob}")

        bqk_c, bqkkey = self.load_cols("L1_attn_b_qk", 128)
        w0c, w0key = self.load_cols("L1_attn_os_weight", 128, slot=0)
        w1c, w1key = self.load_cols("L1_attn_os_weight", 128, slot=1)
        b0c, b0key = self.load_cols("L1_attn_os_bias", 128, slot=0)
        b1c, b1key = self.load_cols("L1_attn_os_bias", 128, slot=1)
        p.op("vector", lambda v: v.tensor_scalar_mul(w0c, w0c, 1.0 / T),
             reads=[w0key], writes=[w0key], tag="w0n")
        p.op("vector", lambda v: v.tensor_scalar_mul(b0c, b0c, 1.0 / T),
             reads=[b0key], writes=[b0key], tag="b0n")
        wqk = []
        for d in range(ND):
            t, key = a.bf16(128, 128, f"awqk{d}")
            p.dma("gpsimd", t, hbm["L1_attn_W_qk"][d * 128:(d + 1) * 128, :],
                  reads=["L1_attn_W_qk@hbm"], writes=[key], tag=f"awqk{d}")
            wqk.append((t, key))
        qkstg, qkskey = a.f32(128, 512, "aqkstg")
        for tb in range(4):
            bank, bkey = self.banks.take()
            for d in range(ND):
                p.op("tensor", lambda t, d=d, tb=tb, b=bank:
                     t.matmul(b, wqk[d][0],
                              xnf[d][0][:, tb * 512:(tb + 1) * 512],
                              start=(d == 0), stop=(d == ND - 1)),
                     reads=[wqk[d][1], xnf[d][1]], writes=[bkey],
                     tag=f"qk tb{tb} d{d}")
            p.op("scalar", lambda s, b=bank: s.activation(
                qkstg, b, AF.Silu, bias=bqk_c),
                reads=[bkey, bqkkey], writes=[qkskey], tag=f"qksilu {tb}")
            p.op("scalar", lambda s, tb=tb: s.activation(
                qt[:, tb * 512:(tb + 1) * 512], qkstg, AF.Identity,
                bias=b0c, scale=w0c),
                reads=[qkskey, b0key, w0key], writes=[qtkey],
                tag=f"qt {tb}")
            p.op("scalar", lambda s, tb=tb: s.activation(
                kt[:, tb * 512:(tb + 1) * 512], qkstg, AF.Identity,
                bias=b1c, scale=w1c),
                reads=[qkskey, b1key, w1key], writes=[ktkey],
                tag=f"kt {tb}")
        a.release()

        a.mark()
        attnT = [a.bf16(128, 512, f"aat{j}") for j in range(NT)]
        rstg = [a.f32(128, 512, f"arstg{k}") for k in range(2)]
        gfm = [a.bf16(128, T, f"agfm{k}") for k in range(2)]
        ogm = [a.bf16(128, 1024, f"aog{c}") for c in range(16)]
        wout = []
        for cix in range(16):
            wout.append(a.bf16(128, 1024, f"awo{cix}"))
        bout_bc, bokey = self.bcast_row("L1_attn_b_out", D)
        x2t2 = [a.f32(128, D, f"bx2{k}") for k in range(2)]
        xpre = [a.f32(128, D, f"bxp{k}") for k in range(2)]
        x3t = [a.f32(128, D, f"bx3{k}") for k in range(2)]
        for cix in range(16):
            t, key = wout[cix]
            p.dma("gpsimd", t,
                  hbm["L1_attn_W_out"][cix * 128:(cix + 1) * 128, :],
                  reads=["L1_attn_W_out@hbm"], writes=[key], tag=f"awo{cix}")

        for ih in range(2):
            for ib2 in range(2):
                ib = ih * 2 + ib2
                for j in range(NT):
                    bank, bkey = self.banks.take()
                    p.op("tensor", lambda t, j=j, ib=ib, b=bank: t.matmul(
                        b, kt[:, j * 128:(j + 1) * 128],
                        qt[:, ib * 512:(ib + 1) * 512], start=True, stop=True),
                        reads=[qtkey, ktkey], writes=[bkey],
                        tag=f"sim ib{ib} j{j}")
                    rs, rskey = rstg[j % 2]
                    p.op("vector", lambda v, b=bank, rs=rs:
                         v.tensor_single_scalar(rs, b, 0.0, ALU.max),
                         reads=[bkey], writes=[rskey],
                         tag=f"relu ib{ib} j{j}")
                    p.op("scalar", lambda s, j=j, rs=rs: s.activation(
                        attnT[j][0], rs, AF.Square),
                        reads=[rskey], writes=[attnT[j][1]],
                        tag=f"sq ib{ib} j{j}")
                for cix in range(16):
                    gf, gfkey = gfm[cix % 2]
                    p.dma("sync", gf,
                          hbm["gate"][:, cix * 128:(cix + 1) * 128],
                          reads=["gate@hbm"], writes=[gfkey], transpose=True,
                          tag=f"gfm ib{ib} c{cix}")
                    bank, bkey = self.banks.take()
                    for j in range(NT):
                        p.op("tensor", lambda t, j=j, cix=cix, b=bank:
                             t.matmul(
                                 b, vtok[j][0][:, cix * 128:(cix + 1) * 128],
                                 attnT[j][0], start=(j == 0),
                                 stop=(j == NT - 1)),
                             reads=[vtok[j][1], attnT[j][1]], writes=[bkey],
                             tag=f"av ib{ib} c{cix} j{j}")
                    p.op("vector", lambda v, b=bank, cix=cix, ib=ib, ib2=ib2,
                         gf=gf:
                         v.tensor_tensor(
                             ogm[cix][0][:, ib2 * 512:(ib2 + 1) * 512],
                             b, gf[:, ib * 512:(ib + 1) * 512], ALU.mult),
                         reads=[bkey, gfkey], writes=[ogm[cix][1]],
                         tag=f"og ib{ib} c{cix}")
            for tc8 in range(8):
                tc = ih * 8 + tc8
                x2, x2key = x2t2[tc % 2]
                xp, xpkey = xpre[tc % 2]
                x3, x3key = x3t[tc % 2]
                p.dma("sync", x2, hbm["x2"][tc * 128:(tc + 1) * 128, :],
                      reads=["x2@hbm"], writes=[x2key], tag=f"bx2 ld {tc}")
                for ob in range(2):
                    bank, bkey = self.banks.take()
                    for cix in range(16):
                        p.op("tensor",
                             lambda t, cix=cix, tc8=tc8, ob=ob, b=bank:
                             t.matmul(
                                 b,
                                 ogm[cix][0][:, tc8 * 128:(tc8 + 1) * 128],
                                 wout[cix][0][:, ob * 512:(ob + 1) * 512],
                                 start=(cix == 0), stop=(cix == 15)),
                             reads=[ogm[cix][1], wout[cix][1]],
                             writes=[bkey], tag=f"wo tc{tc} ob{ob} c{cix}")
                    p.op("vector", lambda v, b=bank, xp=xp, ob=ob:
                         v.tensor_tensor(
                             xp[:, ob * 512:(ob + 1) * 512], b,
                             bout_bc[:, ob * 512:(ob + 1) * 512], ALU.add),
                         reads=[bkey, bokey], writes=[xpkey],
                         tag=f"woev tc{tc} ob{ob}")
                p.op("vector", lambda v, xp=xp, x2=x2:
                     v.tensor_add(xp, xp, x2),
                     reads=[xpkey, x2key], writes=[xpkey], tag=f"bres {tc}")
                self.ln_tile(xp, xpkey, x3, x3key, tag=f"l1o{tc}")
                p.dma("sync", hbm["x3"][tc * 128:(tc + 1) * 128, :], x3,
                      reads=[x3key], writes=["x3@hbm"], tag=f"x3 st {tc}")
                p.dma("gpsimd", hbm["x3bf"][tc * 128:(tc + 1) * 128, :], x3,
                      reads=[x3key], writes=["x3bf@hbm"],
                      tag=f"x3bf st {tc}")
        a.release()
        a.release()


    def sb_row_to_cols(self, row, rowkey, n, name, rows=128):
        a, p = self.a, self.p
        nch = (n + 127) // 128
        cols, ckey = a.f32(128, nch, name)
        identf1 = self.c["identf1"]
        for cix in range(nch):
            w = min(128, n - cix * 128)
            bank, bkey = self.banks.take()
            p.op("tensor", lambda t, c=cix, b=bank, w=w, row=row: t.transpose(
                b[0:w, 0:1], row[0:1, c * 128:c * 128 + w], identf1[0]),
                reads=[rowkey, identf1[1]], writes=[bkey],
                tag=f"sr2c:{name}:{cix}")
            p.op("vector", lambda v, c=cix, b=bank, w=w: v.tensor_copy(
                cols[0:w, c:c + 1], b[0:w, 0:1]),
                reads=[bkey], writes=[ckey], tag=f"sr2c_ev:{name}:{cix}")
        return cols, ckey

    def l1_ffn(self):
        p, a, hbm = self.p, self.a, self.hbm
        NS = 64
        a.mark()

        Ssl, Sslkey = a.f32(64, D, "Sslot")
        p.dma("sync", Ssl, hbm["L1_ffn_slot_init"],
              reads=["L1_ffn_slot_init@hbm"], writes=[Sslkey], tag="S ld")
        Sfm, Sfmkey = a.f32(128, ND * 64, "Sfm")
        Sfb, Sfbkey = a.bf16(128, ND * 64, "Sfb")
        smb, smbkey = a.bf16(128, ND, "smb")
        smf, smfkey = a.f32(128, ND, "smf")
        for c in range(ND):
            bank, bkey = self.banks.take()
            p.op("tensor", lambda t, c=c, b=bank: t.transpose(
                b[0:128, 0:64], Ssl[0:64, c * 128:(c + 1) * 128],
                self.c["identf"][0][0:64, 0:64]),
                reads=[Sslkey, self.c["identf"][1]], writes=[bkey],
                tag=f"ST{c}")
            p.op("vector", lambda v, c=c, b=bank: v.tensor_copy(
                Sfm[:, c * 64:(c + 1) * 64], b[0:128, 0:64]),
                reads=[bkey], writes=[Sfmkey], tag=f"STev{c}")
            p.op("vector", lambda v, c=c, b=bank: v.tensor_copy(
                Sfb[:, c * 64:(c + 1) * 64], b[0:128, 0:64]),
                reads=[bkey], writes=[Sfbkey], tag=f"STevb{c}")
            p.op("vector", lambda v, c=c: v.tensor_scalar(
                smf[:, c:c + 1], Sfm[:, c * 64:(c + 1) * 64], 1.0, 0.0,
                ALU.mult, ALU.add, accum_out=smf[:, c:c + 1])
                if False else v.reduce_sum(
                smf[:, c:c + 1], Sfm[:, c * 64:(c + 1) * 64], axis=AX.X),
                reads=[Sfmkey], writes=[smfkey], tag=f"sm{c}")
        p.op("vector", lambda v: v.tensor_scalar_mul(smf, smf, 1.0 / NS),
             reads=[smfkey], writes=[smfkey], tag="sm scale")
        p.op("vector", lambda v: v.tensor_copy(smb, smf),
             reads=[smfkey], writes=[smbkey], tag="sm cast")

        alphat = [a.bf16(128, 64, f"alft{t}") for t in range(NT)]
        alphaT, alTkey = a.bf16(128, T, "alphaT")
        p.op("vector", lambda v: v.memset(alphaT[64:128, :], 0.0),
             writes=[alTkey], tag="alphaT z")
        x3b = [a.bf16(128, D, f"fx3b{t}") for t in range(NT)]
        for t in range(NT):
            p.dma("sync", x3b[t][0], hbm["x3bf"][t * 128:(t + 1) * 128, :],
                  reads=["x3bf@hbm"], writes=[x3b[t][1]], tag=f"x3b {t}")

        a.mark()
        w1s = []
        for i in range(16):
            t, key = a.bf16(128, 512, f"fw1{i}")
            p.dma("gpsimd", t, hbm["L1_ffn_w1"][i * 128:(i + 1) * 128, :],
                  reads=["L1_ffn_w1@hbm"], writes=[key], tag=f"fw1{i}")
            w1s.append((t, key))
        bank, bkey = self.banks.take()
        for c in range(ND):
            p.op("tensor", lambda t, c=c, b=bank: t.matmul(
                b[0:1, 0:512], smb[:, c:c + 1], w1s[8 + c][0],
                start=(c == 0), stop=(c == ND - 1)),
                reads=[smbkey, w1s[8 + c][1]], writes=[bkey], tag=f"cst{c}")
        b1row, b1key = a.f32(1, 512, "b1row")
        p.dma("sync", b1row, hbm["L1_ffn_b1"][None, :],
              reads=["L1_ffn_b1@hbm"], writes=[b1key], tag="b1 ld")
        cstrow, cstrkey = a.f32(1, 512, "cstrow")
        p.op("vector", lambda v, b=bank: v.tensor_tensor(
            cstrow, b[0:1, 0:512], b1row, ALU.add),
            reads=[bkey, b1key], writes=[cstrkey], tag="cst+b1")
        cstc, cstckey = self.sb_row_to_cols(cstrow, cstrkey, 512, "cstc")

        xf3 = []
        for c in range(ND):
            t, key = a.bf16(128, T, f"fxf{c}")
            p.dma("sync", t, hbm["x3bf"][:, c * 128:(c + 1) * 128],
                  reads=["x3bf@hbm"], writes=[key], transpose=True,
                  tag=f"fxf{c}")
            xf3.append((t, key))
        hfm, hfmkey = a.bf16(128, 4 * T, "fhfm")
        for oc in range(4):
            for tb in range(4):
                bank, bkey = self.banks.take()
                for i in range(ND):
                    p.op("tensor", lambda t, i=i, oc=oc, tb=tb, b=bank:
                         t.matmul(b, w1s[i][0][:, oc * 128:(oc + 1) * 128],
                                  xf3[i][0][:, tb * 512:(tb + 1) * 512],
                                  start=(i == 0), stop=(i == ND - 1)),
                         reads=[w1s[i][1], xf3[i][1]], writes=[bkey],
                         tag=f"fh oc{oc} tb{tb} i{i}")
                p.op("scalar", lambda s, oc=oc, tb=tb, b=bank: s.activation(
                    hfm[:, oc * T + tb * 512:oc * T + (tb + 1) * 512], b,
                    AF.Gelu, bias=cstc[:, oc:oc + 1]),
                    reads=[bkey, cstckey], writes=[hfmkey],
                    tag=f"fhev oc{oc} tb{tb}")
        w2s = []
        for hc in range(4):
            t, key = a.bf16(128, 64, f"fw2{hc}")
            p.dma("gpsimd", t, hbm["L1_ffn_w2"][hc * 128:(hc + 1) * 128, :],
                  reads=["L1_ffn_w2@hbm"], writes=[key], tag=f"fw2{hc}")
            w2s.append((t, key))
        b2c, b2ckey = a.f32(64, 1, "b2c")
        p.dma("sync", b2c, hbm["L1_ffn_b2"].rearrange("(p one) -> p one",
                                                      one=1),
              reads=["L1_ffn_b2@hbm"], writes=[b2ckey], tag="b2 ld")
        lgf, lgfkey = a.f32(64, T, "lgfm")
        for tb in range(4):
            bank, bkey = self.banks.take()
            for hc in range(4):
                p.op("tensor", lambda t, hc=hc, tb=tb, b=bank: t.matmul(
                    b[0:64, 0:512], w2s[hc][0],
                    hfm[:, hc * T + tb * 512:hc * T + (tb + 1) * 512],
                    start=(hc == 0), stop=(hc == 3)),
                    reads=[w2s[hc][1], hfmkey], writes=[bkey],
                    tag=f"lg tb{tb} hc{hc}")
            p.op("scalar", lambda s, tb=tb, b=bank: s.activation(
                lgf[:, tb * 512:(tb + 1) * 512], b[0:64, 0:512], AF.Identity,
                bias=b2c),
                reads=[bkey, b2ckey], writes=[lgfkey], tag=f"lgev tb{tb}")
        tau1, tau1key = a.f32(1, 2, "tau1")
        p.dma("sync", tau1[0:1, 0:1],
              hbm["L1_ffn_tau"][None, :],
              reads=["L1_ffn_tau@hbm"], writes=[tau1key], tag="tau ld")
        p.op("scalar", lambda s: s.activation(tau1[0:1, 1:2], tau1[0:1, 0:1],
                                              AF.Abs),
             reads=[tau1key], writes=[tau1key], tag="tau abs")
        p.op("vector", lambda v: v.tensor_scalar_add(tau1[0:1, 1:2],
                                                     tau1[0:1, 1:2], 0.1),
             reads=[tau1key], writes=[tau1key], tag="tau add")
        p.op("vector", lambda v: v.reciprocal(tau1[0:1, 0:1], tau1[0:1, 1:2]),
             reads=[tau1key], writes=[tau1key], tag="tau rec")
        p.dma("gpsimd", hbm["tauw"][None, :], tau1[0:1, 0:1],
              reads=[tau1key], writes=["tauw@hbm"], tag="tau st")
        tinv, tinvkey = a.f32(128, 1, "tinv")
        p.dma("sync", tinv, hbm["tauw"][None, :].partition_broadcast(128),
              reads=["tauw@hbm"], writes=[tinvkey], tag="tau bc")

        ltok, ltokey = a.f32(128, NT * 64, "ltok")
        for tix in range(NT):
            bank, bkey = self.banks.take()
            p.op("tensor", lambda t, tix=tix, b=bank: t.transpose(
                b[0:128, 0:64], lgf[0:64, tix * 128:(tix + 1) * 128],
                self.c["identf"][0][0:64, 0:64]),
                reads=[lgfkey, self.c["identf"][1]], writes=[bkey],
                tag=f"lT{tix}")
            p.op("vector", lambda v, tix=tix, b=bank: v.tensor_copy(
                ltok[:, tix * 64:(tix + 1) * 64], b[0:128, 0:64]),
                reads=[bkey], writes=[ltokey], tag=f"lTev{tix}")
        work, wkey = a.f32(128, 64, "tkwork")
        eqm, eqkey = a.f32(128, 64, "tkeq")
        mcols, mckey = a.f32(128, 4, "tkm")
        for tix in range(NT):
            lg_t = ltok[:, tix * 64:(tix + 1) * 64]
            p.op("vector", lambda v, lg_t=lg_t: v.tensor_copy(work, lg_t),
                 reads=[ltokey], writes=[wkey], tag=f"tk cp {tix}")
            for it in range(8):
                p.op("vector", lambda v: v.reduce_max(
                    mcols[:, 1:2], work, axis=AX.X),
                    reads=[wkey], writes=[mckey], tag=f"tk max {tix}.{it}")
                if it == 0:
                    p.op("vector", lambda v: v.tensor_copy(mcols[:, 0:1],
                                                           mcols[:, 1:2]),
                         reads=[mckey], writes=[mckey], tag=f"tk m1 {tix}")
                if it < 7:
                    p.op("vector", lambda v: v.tensor_single_scalar(
                        eqm, work, mcols[:, 1:2], ALU.is_equal),
                        reads=[wkey, mckey], writes=[eqkey],
                        tag=f"tk eq {tix}.{it}")
                    p.op("vector", lambda v: v.scalar_tensor_tensor(
                        work, eqm, -1e30, work, ALU.mult, ALU.add),
                        reads=[eqkey, wkey], writes=[wkey],
                        tag=f"tk mask {tix}.{it}")
            p.op("vector", lambda v: v.tensor_mul(mcols[:, 2:3],
                                                  mcols[:, 0:1], tinv),
                 reads=[mckey, tinvkey], writes=[mckey], tag=f"tk b0 {tix}")
            p.op("vector", lambda v: v.tensor_scalar_mul(
                mcols[:, 2:3], mcols[:, 2:3], -1.0),
                reads=[mckey], writes=[mckey], tag=f"tk b1 {tix}")
            p.op("scalar", lambda s, lg_t=lg_t: s.activation(
                work, lg_t, AF.Exp, bias=mcols[:, 2:3], scale=tinv),
                reads=[ltokey, mckey, tinvkey], writes=[wkey],
                tag=f"tk exp {tix}")
            p.op("vector", lambda v, lg_t=lg_t: v.tensor_single_scalar(
                eqm, lg_t, mcols[:, 1:2], ALU.is_ge),
                reads=[ltokey, mckey], writes=[eqkey], tag=f"tk sel {tix}")
            p.op("vector", lambda v: v.tensor_mul(work, work, eqm),
                 reads=[wkey, eqkey], writes=[wkey], tag=f"tk es {tix}")
            p.op("vector", lambda v: v.tensor_scalar(
                eqm, work, 1.0, 0.0, ALU.mult, ALU.add,
                accum_out=mcols[:, 3:4]),
                reads=[wkey], writes=[eqkey, mckey], tag=f"tk Z {tix}")
            p.op("vector", lambda v: v.reciprocal(mcols[:, 3:4],
                                                  mcols[:, 3:4]),
                 reads=[mckey], writes=[mckey], tag=f"tk Zr {tix}")
            p.op("vector", lambda v, tix=tix: v.tensor_scalar_mul(
                alphat[tix][0], work, mcols[:, 3:4]),
                reads=[wkey, mckey], writes=[alphat[tix][1]],
                tag=f"tk al {tix}")
            bank, bkey = self.banks.take()
            p.op("tensor", lambda t, tix=tix, b=bank: t.transpose(
                b.bitcast(BF16)[0:64, 0:128], alphat[tix][0],
                self.c["identb"][0]),
                reads=[alphat[tix][1], self.c["identb"][1]], writes=[bkey],
                tag=f"alT {tix}")
            p.op("vector", lambda v, tix=tix, b=bank: v.tensor_copy(
                alphaT[0:64, tix * 128:(tix + 1) * 128],
                b.bitcast(BF16)[0:64, 0:128]),
                reads=[bkey], writes=[alTkey], tag=f"alTev {tix}")
        a.release()

        a.mark()
        bA, bAkey = self.banks.take()
        bB, bBkey = self.banks.take()
        bD, bDkey = self.banks.take()
        for tix in range(NT):
            p.op("tensor", lambda t, tix=tix: t.matmul(
                bA[0:64, 0:512], alphat[tix][0], x3b[tix][0][:, 0:512],
                start=(tix == 0), stop=(tix == NT - 1)),
                reads=[alphat[tix][1], x3b[tix][1]], writes=[bAkey],
                tag=f"siA {tix}")
            p.op("tensor", lambda t, tix=tix: t.matmul(
                bB[0:64, 0:512], alphat[tix][0], x3b[tix][0][:, 512:1024],
                start=(tix == 0), stop=(tix == NT - 1)),
                reads=[alphat[tix][1], x3b[tix][1]], writes=[bBkey],
                tag=f"siB {tix}")
            p.op("tensor", lambda t, tix=tix: t.matmul(
                bD[0:64, 0:1], alphat[tix][0], self.c["ones_col_bf"][0],
                start=(tix == 0), stop=(tix == NT - 1)),
                reads=[alphat[tix][1], self.c["ones_col_bf"][1]],
                writes=[bDkey], tag=f"siD {tix}")
        rd, rdkey = a.f32(64, 2, "rdenom")
        p.op("vector", lambda v: v.tensor_scalar_add(rd[0:64, 0:1],
                                                     bD[0:64, 0:1], 1e-8),
             reads=[bDkey], writes=[rdkey], tag="denom")
        p.op("vector", lambda v: v.reciprocal(rd[0:64, 1:2], rd[0:64, 0:1]),
             reads=[rdkey], writes=[rdkey], tag="rdenom")
        si, sikey = a.f32(64, D, "si")
        p.op("scalar", lambda s: s.activation(si[0:64, 0:512], bA[0:64, 0:512],
                                              AF.Identity,
                                              scale=rd[0:64, 1:2]),
             reads=[bAkey, rdkey], writes=[sikey], tag="siA ev")
        p.op("scalar", lambda s: s.activation(si[0:64, 512:1024],
                                              bB[0:64, 0:512], AF.Identity,
                                              scale=rd[0:64, 1:2]),
             reads=[bBkey, rdkey], writes=[sikey], tag="siB ev")
        sifb, sifbkey = a.bf16(128, ND * 64, "sifb")
        for c in range(ND):
            bank, bkey = self.banks.take()
            p.op("tensor", lambda t, c=c, b=bank: t.transpose(
                b[0:128, 0:64], si[0:64, c * 128:(c + 1) * 128],
                self.c["identf"][0][0:64, 0:64]),
                reads=[sikey, self.c["identf"][1]], writes=[bkey],
                tag=f"siT{c}")
            p.op("vector", lambda v, c=c, b=bank: v.tensor_copy(
                sifb[:, c * 64:(c + 1) * 64], b[0:128, 0:64]),
                reads=[bkey], writes=[sifbkey], tag=f"siTev{c}")

        p.dma("gpsimd", hbm["wihbf"], hbm["L1_ffn_Wih"],
              reads=["L1_ffn_Wih@hbm"], writes=["wihbf@hbm"], tag="wih cast")
        p.dma("gpsimd", hbm["whhbf"], hbm["L1_ffn_Whh"],
              reads=["L1_ffn_Whh@hbm"], writes=["whhbf@hbm"], tag="whh cast")
        bihc, bihckey = self.load_cols("L1_ffn_bih", 3072)
        bhhc, bhhckey = self.load_cols("L1_ffn_bhh", 3072)
        gi, gikey = a.f32(128, 24 * 64, "gi")
        gh, ghkey = a.f32(128, 24 * 64, "gh")
        a.mark()
        wih = []
        whh = []
        for d in range(ND):
            t, key = a.bf16(128, 3072, f"fwih{d}")
            p.dma("sync", t, hbm["wihbf"][:, d * 128:(d + 1) * 128],
                  reads=["wihbf@hbm"], writes=[key], transpose=True,
                  tag=f"fwih{d}")
            wih.append((t, key))
            t, key = a.bf16(128, 3072, f"fwhh{d}")
            p.dma("sync", t, hbm["whhbf"][:, d * 128:(d + 1) * 128],
                  reads=["whhbf@hbm"], writes=[key], transpose=True,
                  tag=f"fwhh{d}")
            whh.append((t, key))
        for (dst, dstkey, w, s_in, sinkey, bc, bckey) in (
                (gi, gikey, wih, sifb, sifbkey, bihc, bihckey),
                (gh, ghkey, whh, Sfb, Sfbkey, bhhc, bhhckey)):
            for oc in range(24):
                bank, bkey = self.banks.take()
                for d in range(ND):
                    p.op("tensor", lambda t, d=d, oc=oc, b=bank, w=w,
                         s_in=s_in: t.matmul(
                             b[0:128, 0:64],
                             w[d][0][:, oc * 128:(oc + 1) * 128],
                             s_in[:, d * 64:(d + 1) * 64],
                             start=(d == 0), stop=(d == ND - 1)),
                         reads=[w[d][1], sinkey], writes=[bkey],
                         tag=f"gru oc{oc} d{d}")
                p.op("scalar", lambda s, oc=oc, b=bank, dst=dst, bc=bc:
                     s.activation(dst[:, oc * 64:(oc + 1) * 64],
                                  b[0:128, 0:64], AF.Identity,
                                  bias=bc[:, oc:oc + 1]),
                     reads=[bkey, bckey], writes=[dstkey],
                     tag=f"gruev oc{oc}")
        a.release()
        Snf, Snfkey = a.f32(128, ND * 64, "Snf")
        Snb, Snbkey = a.bf16(128, ND * 64, "Snb")
        gt, gtkey = a.f32(128, 2 * 64, "gatetmp")
        for c in range(ND):
            sl = slice(c * 64, (c + 1) * 64)
            slz = slice((8 + c) * 64, (9 + c) * 64)
            sln = slice((16 + c) * 64, (17 + c) * 64)
            p.op("vector", lambda v, sl=sl: v.tensor_add(
                gt[:, 0:64], gi[:, sl], gh[:, sl]),
                reads=[gikey, ghkey], writes=[gtkey], tag=f"r0 {c}")
            p.op("scalar", lambda s: s.activation(gt[:, 0:64], gt[:, 0:64],
                                                  AF.Sigmoid),
                 reads=[gtkey], writes=[gtkey], tag=f"r1 {c}")
            p.op("vector", lambda v, sln=sln: v.tensor_mul(
                gt[:, 64:128], gt[:, 0:64], gh[:, sln]),
                reads=[gtkey, ghkey], writes=[gtkey], tag=f"n0 {c}")
            p.op("vector", lambda v, sln=sln: v.tensor_add(
                gt[:, 64:128], gt[:, 64:128], gi[:, sln]),
                reads=[gtkey, gikey], writes=[gtkey], tag=f"n1 {c}")
            p.op("scalar", lambda s: s.activation(gt[:, 64:128],
                                                  gt[:, 64:128], AF.Tanh),
                 reads=[gtkey], writes=[gtkey], tag=f"n2 {c}")
            p.op("vector", lambda v, slz=slz: v.tensor_add(
                gt[:, 0:64], gi[:, slz], gh[:, slz]),
                reads=[gikey, ghkey], writes=[gtkey], tag=f"z0 {c}")
            p.op("scalar", lambda s: s.activation(gt[:, 0:64], gt[:, 0:64],
                                                  AF.Sigmoid),
                 reads=[gtkey], writes=[gtkey], tag=f"z1 {c}")
            p.op("vector", lambda v, sl=sl: v.tensor_sub(
                Snf[:, sl], Sfm[:, sl], gt[:, 64:128]),
                reads=[Sfmkey, gtkey], writes=[Snfkey], tag=f"sn0 {c}")
            p.op("vector", lambda v, sl=sl: v.tensor_mul(
                Snf[:, sl], Snf[:, sl], gt[:, 0:64]),
                reads=[Snfkey, gtkey], writes=[Snfkey], tag=f"sn1 {c}")
            p.op("vector", lambda v, sl=sl: v.tensor_add(
                Snf[:, sl], Snf[:, sl], gt[:, 64:128]),
                reads=[Snfkey, gtkey], writes=[Snfkey], tag=f"sn2 {c}")
            p.op("vector", lambda v, sl=sl: v.tensor_copy(
                Snb[:, sl], Snf[:, sl]),
                reads=[Snfkey], writes=[Snbkey], tag=f"sn3 {c}")

        DQ = 256
        wq = []
        wk = []
        for d in range(ND):
            t, key = a.bf16(128, DQ, f"fwq{d}")
            p.dma("gpsimd", t, hbm["L1_ffn_wq"][d * 128:(d + 1) * 128, :],
                  reads=["L1_ffn_wq@hbm"], writes=[key], tag=f"fwq{d}")
            wq.append((t, key))
            t, key = a.bf16(128, DQ, f"fwk{d}")
            p.dma("gpsimd", t, hbm["L1_ffn_wk"][d * 128:(d + 1) * 128, :],
                  reads=["L1_ffn_wk@hbm"], writes=[key], tag=f"fwk{d}")
            wk.append((t, key))
        bqc, bqckey = self.load_cols("L1_ffn_bq", DQ)
        bkc, bkckey = self.load_cols("L1_ffn_bk", DQ)
        SC = DQ ** (-0.5)
        p.op("vector", lambda v: v.tensor_scalar_mul(bqc, bqc, SC),
             reads=[bqckey], writes=[bqckey], tag="bq sc")
        qf, qfkey = a.bf16(128, 2 * 64, "qf")
        kf, kfkey = a.bf16(128, 2 * 64, "kf")
        for oc in range(2):
            bank, bkey = self.banks.take()
            for d in range(ND):
                p.op("tensor", lambda t, d=d, oc=oc, b=bank: t.matmul(
                    b[0:128, 0:64], wq[d][0][:, oc * 128:(oc + 1) * 128],
                    Snb[:, d * 64:(d + 1) * 64],
                    start=(d == 0), stop=(d == ND - 1)),
                    reads=[wq[d][1], Snbkey], writes=[bkey],
                    tag=f"q oc{oc} d{d}")
            p.op("scalar", lambda s, oc=oc, b=bank: s.activation(
                qf[:, oc * 64:(oc + 1) * 64], b[0:128, 0:64], AF.Identity,
                bias=bqc[:, oc:oc + 1], scale=SC),
                reads=[bkey, bqckey], writes=[qfkey], tag=f"qev oc{oc}")
            bank, bkey = self.banks.take()
            for d in range(ND):
                p.op("tensor", lambda t, d=d, oc=oc, b=bank: t.matmul(
                    b[0:128, 0:64], wk[d][0][:, oc * 128:(oc + 1) * 128],
                    Snb[:, d * 64:(d + 1) * 64],
                    start=(d == 0), stop=(d == ND - 1)),
                    reads=[wk[d][1], Snbkey], writes=[bkey],
                    tag=f"k oc{oc} d{d}")
            p.op("scalar", lambda s, oc=oc, b=bank: s.activation(
                kf[:, oc * 64:(oc + 1) * 64], b[0:128, 0:64], AF.Identity,
                bias=bkc[:, oc:oc + 1]),
                reads=[bkey, bkckey], writes=[kfkey], tag=f"kev oc{oc}")
        wv = []
        for d in range(ND):
            t, key = a.bf16(128, D, f"fwv{d}")
            p.dma("gpsimd", t, hbm["L1_ffn_wv"][d * 128:(d + 1) * 128, :],
                  reads=["L1_ffn_wv@hbm"], writes=[key], tag=f"fwv{d}")
            wv.append((t, key))
        bv_bc, bvkey = self.bcast_row("L1_ffn_bv", D, rows=64)
        vsl, vslkey = a.bf16(128, D, "vslot")
        p.op("vector", lambda v: v.memset(vsl[64:128, :], 0.0),
             writes=[vslkey], tag="v z")
        for ob in range(2):
            bank, bkey = self.banks.take()
            for d in range(ND):
                p.op("tensor", lambda t, d=d, ob=ob, b=bank: t.matmul(
                    b[0:64, 0:512], Snb[:, d * 64:(d + 1) * 64],
                    wv[d][0][:, ob * 512:(ob + 1) * 512],
                    start=(d == 0), stop=(d == ND - 1)),
                    reads=[Snbkey, wv[d][1]], writes=[bkey],
                    tag=f"v ob{ob} d{d}")
            p.op("vector", lambda v, ob=ob, b=bank: v.tensor_tensor(
                vsl[0:64, ob * 512:(ob + 1) * 512], b[0:64, 0:512],
                bv_bc[0:64, ob * 512:(ob + 1) * 512], ALU.add),
                reads=[bkey, bvkey], writes=[vslkey], tag=f"vev ob{ob}")

        sim, simkey = a.f32(64, 80, "sim")
        bank, bkey = self.banks.take()
        for oc in range(2):
            p.op("tensor", lambda t, oc=oc, b=bank: t.matmul(
                b[0:64, 0:64], qf[:, oc * 64:(oc + 1) * 64],
                kf[:, oc * 64:(oc + 1) * 64],
                start=(oc == 0), stop=(oc == 1)),
                reads=[qfkey, kfkey], writes=[bkey], tag=f"sim oc{oc}")
        p.op("vector", lambda v, b=bank: v.tensor_copy(sim[0:64, 0:64],
                                                       b[0:64, 0:64]),
             reads=[bkey], writes=[simkey], tag="sim ev")
        p.op("vector", lambda v: v.reduce_max(sim[0:64, 64:65],
                                              sim[0:64, 0:64], axis=AX.X),
             reads=[simkey], writes=[simkey], tag="sim max")
        p.op("vector", lambda v: v.tensor_scalar_mul(sim[0:64, 64:65],
                                                     sim[0:64, 64:65], -1.0),
             reads=[simkey], writes=[simkey], tag="sim neg")
        p.op("scalar", lambda s: s.activation(sim[0:64, 0:64],
                                              sim[0:64, 0:64], AF.Exp,
                                              bias=sim[0:64, 64:65]),
             reads=[simkey], writes=[simkey], tag="sim exp")
        p.op("vector", lambda v: v.tensor_scalar(
            sim[0:64, 66:67], sim[0:64, 0:64], 1.0, 0.0, ALU.mult, ALU.add,
            accum_out=sim[0:64, 65:66]) if False else v.reduce_sum(
            sim[0:64, 65:66], sim[0:64, 0:64], axis=AX.X),
            reads=[simkey], writes=[simkey], tag="sim Z")
        p.op("vector", lambda v: v.reciprocal(sim[0:64, 65:66],
                                              sim[0:64, 65:66]),
             reads=[simkey], writes=[simkey], tag="sim Zr")
        p.op("vector", lambda v: v.tensor_scalar_mul(
            sim[0:64, 0:64], sim[0:64, 0:64], sim[0:64, 65:66]),
            reads=[simkey], writes=[simkey], tag="sim n")
        Pb, Pbkey = a.bf16(64, 64, "Pb")
        p.op("vector", lambda v: v.tensor_copy(Pb, sim[0:64, 0:64]),
             reads=[simkey], writes=[Pbkey], tag="P cast")
        PT, PTkey = a.bf16(128, 64, "PT")
        p.op("vector", lambda v: v.memset(PT[64:128, :], 0.0),
             writes=[PTkey], tag="PT z")
        bank, bkey = self.banks.take()
        p.op("tensor", lambda t, b=bank: t.transpose(
            b.bitcast(BF16)[0:64, 0:64], Pb, self.c["identb"][0][0:64, 0:64]),
            reads=[Pbkey, self.c["identb"][1]], writes=[bkey], tag="PT T")
        p.op("vector", lambda v, b=bank: v.tensor_copy(
            PT[0:64, :], b.bitcast(BF16)[0:64, 0:64]),
            reads=[bkey], writes=[PTkey], tag="PT ev")

        Snsl, Snslkey = a.f32(64, D, "Snslot")
        for c in range(ND):
            bank, bkey = self.banks.take()
            p.op("tensor", lambda t, c=c, b=bank: t.transpose(
                b[0:64, 0:128], Snf[:, c * 64:(c + 1) * 64],
                self.c["identf"][0]),
                reads=[Snfkey, self.c["identf"][1]], writes=[bkey],
                tag=f"SnT{c}")
            p.op("vector", lambda v, c=c, b=bank: v.tensor_copy(
                Snsl[0:64, c * 128:(c + 1) * 128], b[0:64, 0:128]),
                reads=[bkey], writes=[Snslkey], tag=f"SnTev{c}")
        y, ykey = a.f32(64, D, "y")
        for ob in range(2):
            bank, bkey = self.banks.take()
            p.op("tensor", lambda t, ob=ob, b=bank: t.matmul(
                b[0:64, 0:512], PT, vsl[:, ob * 512:(ob + 1) * 512],
                start=True, stop=True),
                reads=[PTkey, vslkey], writes=[bkey], tag=f"scm ob{ob}")
            p.op("vector", lambda v, ob=ob, b=bank: v.tensor_tensor(
                y[0:64, ob * 512:(ob + 1) * 512], b[0:64, 0:512],
                Snsl[0:64, ob * 512:(ob + 1) * 512], ALU.add),
                reads=[bkey, Snslkey], writes=[ykey], tag=f"y ob{ob}")
        yn, ynkey = a.f32(64, D, "yn")
        self.ln_tile(y, ykey, yn, ynkey, rows=64, tag="l1f")
        lng_bc, lngkey = self.bcast_row("L1_ffn_ln_g", D, rows=64)
        lnb_bc, lnbkey = self.bcast_row("L1_ffn_ln_b", D, rows=64)
        p.op("vector", lambda v: v.tensor_mul(yn, yn, lng_bc),
             reads=[ynkey, lngkey], writes=[ynkey], tag="yg")
        p.op("vector", lambda v: v.tensor_add(yn, yn, lnb_bc),
             reads=[ynkey, lnbkey], writes=[ynkey], tag="yb")
        ynfb, ynfbkey = a.bf16(128, ND * 64, "ynfb")
        for c in range(ND):
            bank, bkey = self.banks.take()
            p.op("tensor", lambda t, c=c, b=bank: t.transpose(
                b[0:128, 0:64], yn[0:64, c * 128:(c + 1) * 128],
                self.c["identf"][0][0:64, 0:64]),
                reads=[ynkey, self.c["identf"][1]], writes=[bkey],
                tag=f"ynT{c}")
            p.op("vector", lambda v, c=c, b=bank: v.tensor_copy(
                ynfb[:, c * 64:(c + 1) * 64], b[0:128, 0:64]),
                reads=[bkey], writes=[ynfbkey], tag=f"ynTev{c}")

        wvp = []
        for d in range(ND):
            t, key = a.bf16(128, D, f"fwvp{d}")
            p.dma("gpsimd", t, hbm["L1_ffn_wvp"][d * 128:(d + 1) * 128, :],
                  reads=["L1_ffn_wvp@hbm"], writes=[key], tag=f"fwvp{d}")
            wvp.append((t, key))
        bvpc, bvpckey = self.load_cols("L1_ffn_bvp", D)
        svf, svfkey = a.f32(128, 64, "svf")
        svb, svbkey = a.bf16(128, 64, "svb")
        svsl, svslkey = a.bf16(128, D, "svslot")
        p.op("vector", lambda v: v.memset(svsl[64:128, :], 0.0),
             writes=[svslkey], tag="sv z")
        for oc in range(ND):
            bank, bkey = self.banks.take()
            for d in range(ND):
                p.op("tensor", lambda t, d=d, oc=oc, b=bank: t.matmul(
                    b[0:128, 0:64], wvp[d][0][:, oc * 128:(oc + 1) * 128],
                    ynfb[:, d * 64:(d + 1) * 64],
                    start=(d == 0), stop=(d == ND - 1)),
                    reads=[wvp[d][1], ynfbkey], writes=[bkey],
                    tag=f"sv oc{oc} d{d}")
            p.op("scalar", lambda s, oc=oc, b=bank: s.activation(
                svf, b[0:128, 0:64], AF.Identity, bias=bvpc[:, oc:oc + 1]),
                reads=[bkey, bvpckey], writes=[svfkey], tag=f"svev oc{oc}")
            p.op("vector", lambda v: v.tensor_copy(svb, svf),
                 reads=[svfkey], writes=[svbkey], tag=f"svc oc{oc}")
            bank, bkey = self.banks.take()
            p.op("tensor", lambda t, oc=oc, b=bank: t.transpose(
                b.bitcast(BF16)[0:64, 0:128], svb, self.c["identb"][0]),
                reads=[svbkey, self.c["identb"][1]], writes=[bkey],
                tag=f"svT oc{oc}")
            p.op("vector", lambda v, oc=oc, b=bank: v.tensor_copy(
                svsl[0:64, oc * 128:(oc + 1) * 128],
                b.bitcast(BF16)[0:64, 0:128]),
                reads=[bkey], writes=[svslkey], tag=f"svTev oc{oc}")
        a.release()

        a.mark()
        OF = [a.bf16(128, T, f"fOF{c}") for c in range(ND)]
        for dc in range(ND):
            for tb in range(4):
                bank, bkey = self.banks.take()
                p.op("tensor", lambda t, dc=dc, tb=tb, b=bank: t.matmul(
                    b, svsl[:, dc * 128:(dc + 1) * 128],
                    alphaT[:, tb * 512:(tb + 1) * 512],
                    start=True, stop=True),
                    reads=[svslkey, alTkey], writes=[bkey],
                    tag=f"OF dc{dc} tb{tb}")
                p.op("vector", lambda v, dc=dc, tb=tb, b=bank: v.tensor_copy(
                    OF[dc][0][:, tb * 512:(tb + 1) * 512], b),
                    reads=[bkey], writes=[OF[dc][1]], tag=f"OFev dc{dc} tb{tb}")
        wo = []
        for d in range(ND):
            t, key = a.bf16(128, D, f"fwo{d}")
            p.dma("gpsimd", t, hbm["L1_ffn_wo"][d * 128:(d + 1) * 128, :],
                  reads=["L1_ffn_wo@hbm"], writes=[key], tag=f"fwo{d}")
            wo.append((t, key))
        bo_bc, bokey = self.bcast_row("L1_ffn_bo", D)
        x3t = [a.f32(128, D, f"cx3{k}") for k in range(2)]
        xpre = [a.f32(128, D, f"cxp{k}") for k in range(2)]
        x4t = [a.f32(128, D, f"cx4{k}") for k in range(2)]
        for tc in range(NT):
            x3, x3key = x3t[tc % 2]
            xp, xpkey = xpre[tc % 2]
            x4, x4key = x4t[tc % 2]
            p.dma("sync", x3, hbm["x3"][tc * 128:(tc + 1) * 128, :],
                  reads=["x3@hbm"], writes=[x3key], tag=f"cx3 ld {tc}")
            for ob in range(2):
                bank, bkey = self.banks.take()
                for dc in range(ND):
                    p.op("tensor", lambda t, dc=dc, tc=tc, ob=ob, b=bank:
                         t.matmul(b, OF[dc][0][:, tc * 128:(tc + 1) * 128],
                                  wo[dc][0][:, ob * 512:(ob + 1) * 512],
                                  start=(dc == 0), stop=(dc == ND - 1)),
                         reads=[OF[dc][1], wo[dc][1]], writes=[bkey],
                         tag=f"fwo tc{tc} ob{ob} dc{dc}")
                p.op("vector", lambda v, b=bank, xp=xp, ob=ob: v.tensor_tensor(
                    xp[:, ob * 512:(ob + 1) * 512], b,
                    bo_bc[:, ob * 512:(ob + 1) * 512], ALU.add),
                    reads=[bkey, bokey], writes=[xpkey],
                    tag=f"fwoev tc{tc} ob{ob}")
            p.op("vector", lambda v, xp=xp, x3=x3: v.tensor_add(xp, xp, x3),
                 reads=[xpkey, x3key], writes=[xpkey], tag=f"cres {tc}")
            self.ln_tile(xp, xpkey, x4, x4key, tag=f"l1f{tc}")
            p.dma("sync", hbm["out"][tc * 128:(tc + 1) * 128, :], x4,
                  reads=[x4key], writes=["out@hbm"], tag=f"out st {tc}")
        a.release()
        a.release()



def build_program(upto=4):
    nc = bass.Bass("TRN2", target_bir_lowering=False)
    hbm = {}
    shapes = {
        "L0_attn_x": ([T, D], F32),
        "L0_attn_w_qkv": ([D, 3 * D], F32),
        "L0_attn_b_qkv": ([3 * D], F32),
        "L0_attn_w_proj": ([D, D], F32),
        "L0_attn_b_proj": ([D], F32),
        "L0_ffn_w1_w": ([4096, D], F32),
        "L0_ffn_w1_s": ([4096], F32),
        "L0_ffn_w2_w": ([4096, D], F32),
        "L0_ffn_w2_s": ([4096], F32),
        "L0_ffn_h1_w": ([4096, 4096], F32),
        "L0_ffn_h1_s": ([4096], F32),
        "L0_ffn_h2_w": ([4096, 4096], F32),
        "L0_ffn_h2_s": ([4096], F32),
        "L0_ffn_w3_w": ([D, 4096], F32),
        "L0_ffn_w3_s": ([D], F32),
        "L1_attn_ln_scale": ([D], F32),
        "L1_attn_ln_bias": ([D], F32),
        "L1_attn_W_hidden": ([D, 4096], F32),
        "L1_attn_b_hidden": ([4096], F32),
        "L1_attn_W_qk": ([D, 128], F32),
        "L1_attn_b_qk": ([128], F32),
        "L1_attn_os_weight": ([2, 128], F32),
        "L1_attn_os_bias": ([2, 128], F32),
        "L1_attn_W_out": ([2 * D, D], F32),
        "L1_attn_b_out": ([D], F32),
        "L1_ffn_slot_init": ([64, D], F32),
        "L1_ffn_w1": ([2 * D, 512], F32),
        "L1_ffn_b1": ([512], F32),
        "L1_ffn_w2": ([512, 64], F32),
        "L1_ffn_b2": ([64], F32),
        "L1_ffn_tau": ([1], F32),
        "L1_ffn_Wih": ([3 * D, D], F32),
        "L1_ffn_Whh": ([3 * D, D], F32),
        "L1_ffn_bih": ([3 * D], F32),
        "L1_ffn_bhh": ([3 * D], F32),
        "L1_ffn_wq": ([D, 256], F32),
        "L1_ffn_bq": ([256], F32),
        "L1_ffn_wk": ([D, 256], F32),
        "L1_ffn_bk": ([256], F32),
        "L1_ffn_wv": ([D, D], F32),
        "L1_ffn_bv": ([D], F32),
        "L1_ffn_ln_g": ([D], F32),
        "L1_ffn_ln_b": ([D], F32),
        "L1_ffn_wvp": ([D, D], F32),
        "L1_ffn_bvp": ([D], F32),
        "L1_ffn_wo": ([D, D], F32),
        "L1_ffn_bo": ([D], F32),
    }
    for name, (shape, dt) in shapes.items():
        hbm[name] = nc.dram_tensor(name, shape, dt, kind="ExternalInput").ap()
    hbm["out"] = nc.dram_tensor("out", [T, D], F32, kind="ExternalOutput").ap()

    scratch = {
        "x0bf": ([T, D], BF16), "x1": ([T, D], F32), "x1bf": ([T, D], BF16),
        "x2": ([T, D], F32), "x2bf": ([T, D], BF16),
        "x3": ([T, D], F32), "x3bf": ([T, D], BF16),
        "vfm": ([D, T], BF16), "rrow": ([T], F32),
        "w1bf": ([4096, D], BF16), "w2bf": ([4096, D], BF16),
        "w3bf": ([D, 4096], BF16),
        "h1bf": ([4096, 4096], BF16), "h2bf": ([4096, 4096], BF16),
        "h": ([T, 4096], BF16), "g1": ([T, 4096], BF16),
        "g2": ([T, 4096], BF16),
        "xnbf": ([T, D], BF16), "gate": ([T, 2048], BF16),
        "tauw": ([1], F32), "wihbf": ([3072, D], BF16),
        "whhbf": ([3072, D], BF16),
    }
    for name, (shape, dt) in scratch.items():
        hbm[name] = nc.dram_tensor("scr_" + name, shape, dt,
                                   kind="Internal").ap()

    words = 53000
    arena_t = nc.alloc_sbuf_tensor("arena", [128, words], F32)
    arena = Arena(arena_t.ap(), words)

    bank_aps = []
    bank_handles = []
    for i in range(8):
        h = nc.alloc_psum_tensor(f"bank{i}", [128, 512], F32)
        bank_handles.append(h)
        bank_aps.append((h.ap(), f"psum{i}"))
    banks = Banks(bank_aps)

    prog = Prog(nc)
    b = Builder(nc, prog, arena, banks, hbm)
    b.prelude()
    if upto >= 1:
        b.l0_attn()
    if upto >= 2:
        b.l0_ffn()
    if upto >= 3:
        b.l1_attn()
    if upto >= 4:
        b.l1_ffn()
    final = {1: "x1", 2: "x2", 3: "x3", 4: "x3"}[upto]
    if upto < 4:
        prog.dma("sync", hbm["out"].rearrange("a b -> (a b)")[None, :],
                 hbm[final].rearrange("a b -> (a b)")[None, :],
                 reads=[final + "@hbm"], writes=["out@hbm"], tag="final cp")

    comp_sems = {}
    dma_sems = {}
    import contextlib
    stack = contextlib.ExitStack()
    for e in ENGINES:
        comp_sems[e] = stack.enter_context(nc.semaphore(f"c_{e}"))
        dma_sems[e] = stack.enter_context(nc.semaphore(f"d_{e}"))
    n, ms, dc = prog.emit(comp_sems, dma_sems)
    stack.close()
    return nc, n, ms, dc, arena.peak


def kernel(**inputs):
    x = np.ascontiguousarray(inputs["L0_attn_x"])
    nc, n, ms, dc, peak = build_program(upto=4)
    weights = {k: np.ascontiguousarray(v) for k, v in inputs.items()
               if k != "L0_attn_x"}
    in_maps = []
    for core in range(8):
        m = dict(weights)
        m["L0_attn_x"] = x[core]
        in_maps.append(m)
    res = run_bass_kernel_spmd(nc, in_maps, core_ids=list(range(8)))
    return np.stack([r["out"] for r in res.results], axis=0)


if __name__ == "__main__":
    nc, n, ms, dc, peak = build_program(upto=1)
    print("instructions:", n, "milestones:", ms, "dmas:", dc,
          "arena peak words:", peak)


# version 02
# speedup vs baseline: 1.0379x; 1.0379x over previous
import numpy as np

import concourse.bass as bass
import concourse.mybir as mybir
from concourse.bass_utils import run_bass_kernel_spmd

F32 = mybir.dt.float32
BF16 = mybir.dt.bfloat16
AF = mybir.ActivationFunctionType
ALU = mybir.AluOpType
AX = mybir.AxisListType

ENGINES = ("sync", "scalar", "vector", "tensor", "gpsimd")

T = 2048
D = 1024
NT = T // 128
ND = D // 128



class _Op:
    __slots__ = ("idx", "engine", "fn", "is_dma", "deps", "milestone",
                 "ms_no", "dma_no", "tag", "waits")

    def __init__(self, idx, engine, fn, is_dma, tag):
        self.idx = idx
        self.engine = engine
        self.fn = fn
        self.is_dma = is_dma
        self.deps = []
        self.milestone = False
        self.ms_no = None
        self.dma_no = None
        self.tag = tag
        self.waits = []


class Prog:
    def __init__(self, nc):
        self.nc = nc
        self.ops = []
        self.last_writer = {}
        self.readers = {}

    def op(self, engine, fn, reads=(), writes=(), tag=""):
        o = _Op(len(self.ops), engine, fn, False, tag)
        self._record(o, reads, writes)
        return o

    def dma(self, qengine, out, in_, reads=(), writes=(), transpose=False,
            tag=""):
        if transpose:
            fn = lambda eng: eng.dma_start_transpose(out=out, in_=in_)
        else:
            fn = lambda eng: eng.dma_start(out=out, in_=in_)
        o = _Op(len(self.ops), qengine, fn, True, tag)
        self._record(o, reads, writes)
        return o

    @staticmethod
    def _flatten(keys):
        out = []
        for k in keys:
            if isinstance(k, str):
                out.append(k)
            else:
                out.extend(k)
        return out

    def _record(self, o, reads, writes):
        reads = self._flatten(reads)
        writes = self._flatten(writes)
        deps = set()
        for r in reads:
            w = self.last_writer.get(r)
            if w is not None:
                deps.add(w)
        for w_ in writes:
            lw = self.last_writer.get(w_)
            if lw is not None:
                deps.add(lw)
            for rd in self.readers.get(w_, ()):
                deps.add(rd)
        for r in reads:
            self.readers.setdefault(r, []).append(o)
        for w_ in writes:
            self.last_writer[w_] = o
            self.readers[w_] = [o]
        deps.discard(o)
        o.deps = sorted(deps, key=lambda d: d.idx)
        self.ops.append(o)

    @staticmethod
    def _edge_needed(dep, o):
        if dep is o:
            return False
        if (dep.engine == "tensor" and o.engine == "tensor"
                and not dep.is_dma and not o.is_dma):
            return False
        if o.is_dma and not dep.is_dma and dep.engine == o.engine:
            return False
        return True

    def emit(self, comp_sems, dma_sems):
        for o in self.ops:
            best = {}
            for d in o.deps:
                if not self._edge_needed(d, o):
                    continue
                key = (d.engine, d.is_dma)
                if key not in best or best[key].idx < d.idx:
                    best[key] = d
            o.deps = sorted(best.values(), key=lambda d: d.idx)
            for d in o.deps:
                if not d.is_dma:
                    d.milestone = True
        ms_count = {e: 0 for e in ENGINES}
        dma_count = {e: 0 for e in ENGINES}
        for o in self.ops:
            if o.is_dma:
                dma_count[o.engine] += 1
                o.dma_no = dma_count[o.engine]
            elif o.milestone:
                ms_count[o.engine] += 1
                o.ms_no = ms_count[o.engine]
        for e in ENGINES:
            assert ms_count[e] < 60000, (e, ms_count[e])
            assert dma_count[e] * 16 < 60000, (e, dma_count[e])

        state = {e: {} for e in ENGINES}
        for o in self.ops:
            waits = {}
            for d in o.deps:
                if not self._edge_needed(d, o):
                    continue
                if d.is_dma:
                    key, val = ("dma", d.engine), 16 * d.dma_no
                else:
                    key, val = ("comp", d.engine), d.ms_no
                if waits.get(key, 0) < val:
                    waits[key] = val
            st = state[o.engine]
            for key, val in sorted(waits.items()):
                if st.get(key, 0) >= val:
                    continue
                st[key] = val
                o.waits.append((key, val))

        per_engine = {e: [o for o in self.ops if o.engine == e]
                      for e in ENGINES}
        counts = {"n": 0}

        def emit_stream(eng_name):
            def run(eng):
                for o in per_engine[eng_name]:
                    for (kind, src), val in o.waits:
                        sem = dma_sems[src] if kind == "dma" else comp_sems[src]
                        eng.wait_ge(sem, val)
                    inst = o.fn(eng)
                    if o.is_dma:
                        inst.then_inc(dma_sems[eng_name], 16)
                    elif o.milestone:
                        assert inst is not None, (o.tag, o.engine)
                        inst.then_inc(comp_sems[eng_name], 1)
                    counts["n"] += 1
                if eng_name == "sync":
                    for e2 in ENGINES:
                        if dma_count[e2]:
                            eng.wait_ge(dma_sems[e2], 16 * dma_count[e2])
                        if ms_count[e2] and e2 != "sync":
                            eng.wait_ge(comp_sems[e2], ms_count[e2])
            return run

        with self.nc.Block() as block:
            block.sync(emit_stream("sync"))
            block.scalar(emit_stream("scalar"))
            block.vector(emit_stream("vector"))
            block.tensor(emit_stream("tensor"))
            block.gpsimd(emit_stream("gpsimd"))
        return counts["n"], ms_count, dma_count



class Arena:
    def __init__(self, ap, n_words):
        self.ap = ap
        self.n = n_words
        self.off = 0
        self.marks = []
        self.peak = 0

    def _take(self, words, align=16):
        off = (self.off + align - 1) // align * align
        assert off + words <= self.n, \
            f"arena overflow: {off}+{words} > {self.n}"
        self.off = off + words
        self.peak = max(self.peak, self.off)
        return off

    PAGE = 256

    def _pages(self, off, words):
        return tuple(f"a{pg}" for pg in range(off // self.PAGE,
                                              (off + words - 1) // self.PAGE + 1))

    def f32(self, rows, cols, name):
        off = self._take(cols)
        return self.ap[0:rows, off:off + cols], self._pages(off, cols)

    def bf16(self, rows, cols, name):
        words = (cols + 1) // 2
        off = self._take(words)
        ap = self.ap[:, off:off + words].bitcast(BF16)[0:rows, 0:cols]
        return ap, self._pages(off, words)

    def mark(self):
        self.marks.append(self.off)

    def release(self):
        self.off = self.marks.pop()


class Banks:
    def __init__(self, pairs):
        self.pairs = pairs
        self.i = 0

    def take(self):
        ap, key = self.pairs[self.i % len(self.pairs)]
        self.i += 1
        return ap, key



class Builder:
    def __init__(self, nc, prog, arena, banks, hbm):
        self.nc = nc
        self.p = prog
        self.a = arena
        self.banks = banks
        self.hbm = hbm
        self.c = {}


    def prelude(self):
        a, p = self.a, self.p
        iota, iokey = a.f32(128, 128, "iota")
        p.op("gpsimd", lambda g: g.iota(iota, pattern=[[-1, 128]], base=0,
                                        channel_multiplier=1,
                                        allow_small_or_imprecise_dtypes=True),
             writes=[iokey], tag="iota")
        identf, ifkey = a.f32(128, 128, "identf")
        p.op("vector", lambda v: v.tensor_single_scalar(identf, iota, 0.0,
                                                        ALU.is_equal),
             reads=[iokey], writes=[ifkey], tag="identf")
        identb, ibkey = a.bf16(128, 128, "identb")
        p.op("vector", lambda v: v.tensor_copy(identb, identf),
             reads=[ifkey], writes=[ibkey], tag="identb")
        self.c["identf"] = (identf, ifkey)
        self.c["identb"] = (identb, ibkey)
        i1, i1key = a.f32(1, 1, "identf1")
        p.op("vector", lambda v: v.memset(i1, 1.0), writes=[i1key],
             tag="identf1")
        self.c["identf1"] = (i1, i1key)
        eps, ekey = a.f32(128, 1, "eps_col")
        p.op("vector", lambda v: v.memset(eps, 1e-5), writes=[ekey],
             tag="eps")
        self.c["eps_col"] = (eps, ekey)
        ones_bf, obkey = a.bf16(1, T, "ones_row_bf")
        p.op("vector", lambda v: v.memset(ones_bf, 1.0), writes=[obkey],
             tag="ones_row_bf")
        self.c["ones_row_bf"] = (ones_bf, obkey)
        ones_col_bf, ockey = a.bf16(128, 1, "ones_col_bf")
        p.op("vector", lambda v: v.memset(ones_col_bf, 1.0), writes=[ockey],
             tag="ones_col_bf")
        self.c["ones_col_bf"] = (ones_col_bf, ockey)
        junk, jkey = a.bf16(128, 1024, "junk")
        self.c["junk"] = (junk, jkey)
        junk2, j2key = a.bf16(128, 1024, "junk2")
        self.c["junk2"] = (junk2, j2key)
        self._ln_ctr = 0


    def load_row(self, hbm_name, n, name, slot=None):
        src = self.hbm[hbm_name]
        if slot is not None:
            src = src[slot]
        row, key = self.a.f32(1, n, name)
        self.p.dma("sync", row, src[None, :], reads=[hbm_name + "@hbm"],
                   writes=[key], tag="load_row:" + name)
        return row, key

    def row_to_cols(self, row, rowkey, n, name):
        nch = (n + 127) // 128
        cols, ckey = self.a.f32(128, nch, name)
        identf1 = self.c["identf1"]
        for cix in range(nch):
            w = min(128, n - cix * 128)
            bank, bkey = self.banks.take()
            self.p.op(
                "tensor",
                lambda t, c=cix, b=bank, w=w: t.transpose(
                    b[0:w, 0:1], row[0:1, c * 128:c * 128 + w], identf1[0]),
                reads=[rowkey, identf1[1]], writes=[bkey],
                tag=f"r2c:{name}:{cix}")
            self.p.op(
                "vector",
                lambda v, c=cix, b=bank, w=w: v.tensor_copy(
                    cols[0:w, c:c + 1], b[0:w, 0:1]),
                reads=[bkey], writes=[ckey], tag=f"r2c_ev:{name}:{cix}")
        return cols, ckey

    def load_cols(self, hbm_name, n, slot=None):
        nch = (n + 127) // 128
        cols, ckey = self.a.f32(128, nch, hbm_name + "_cols")
        self.a.mark()
        row, rkey = self.a.f32(1, n, hbm_name + "_row")
        src = self.hbm[hbm_name]
        if slot is not None:
            src = src[slot]
        self.p.dma("sync", row, src[None, :],
                   reads=[hbm_name + "@hbm"], writes=[rkey],
                   tag="load_row:" + hbm_name)
        identf1 = self.c["identf1"]
        for cix in range(nch):
            w = min(128, n - cix * 128)
            bank, bkey = self.banks.take()
            self.p.op(
                "tensor",
                lambda t, c=cix, b=bank, w=w, row=row: t.transpose(
                    b[0:w, 0:1], row[0:1, c * 128:c * 128 + w], identf1[0]),
                reads=[rkey, identf1[1]], writes=[bkey],
                tag=f"r2c:{hbm_name}:{cix}")
            self.p.op(
                "vector",
                lambda v, c=cix, b=bank, w=w: v.tensor_copy(
                    cols[0:w, c:c + 1], b[0:w, 0:1]),
                reads=[bkey], writes=[ckey], tag=f"r2c_ev:{hbm_name}:{cix}")
        self.a.release()
        return cols, ckey

    def bcast_row(self, hbm_name, n, rows=128, scale=None):
        t, key = self.a.f32(rows, n, hbm_name + "_bc")
        self.p.dma("sync", t,
                   self.hbm[hbm_name][None, :].partition_broadcast(rows),
                   reads=[hbm_name + "@hbm"], writes=[key],
                   tag="bcast:" + hbm_name)
        return t, key

    def ln_tile(self, x_ap, xkey, out_ap, outkey, rows=128, width=D, tag=""):
        a, p = self.a, self.p
        a.mark()
        self._ln_ctr += 1
        junk, jkey = self.c["junk" if self._ln_ctr % 2 else "junk2"]
        s12, skey = a.f32(rows, 2, f"lns{tag}")
        cols, ckey = a.f32(rows, 4, f"lnc{tag}")
        p.op("vector", lambda v: v.tensor_scalar(
            junk[0:rows, 0:width], x_ap, 1.0, 0.0, ALU.mult, ALU.add,
            accum_out=s12[0:rows, 0:1]),
            reads=[xkey], writes=[skey, jkey], tag=f"ln_s1{tag}")
        p.op("scalar", lambda s: s.activation(
            junk[0:rows, 0:width], x_ap, AF.Square,
            accum_out=s12[0:rows, 1:2]),
            reads=[xkey], writes=[skey, jkey], tag=f"ln_s2{tag}")
        inv = 1.0 / width
        p.op("vector", lambda v: v.tensor_scalar_mul(
            cols[0:rows, 0:1], s12[0:rows, 0:1], inv),
            reads=[skey], writes=[ckey], tag=f"ln_mu{tag}")
        p.op("vector", lambda v: v.tensor_scalar_mul(
            cols[0:rows, 2:3], s12[0:rows, 1:2], inv),
            reads=[skey], writes=[ckey], tag=f"ln_t0{tag}")
        p.op("vector", lambda v: v.tensor_mul(
            cols[0:rows, 3:4], cols[0:rows, 0:1], cols[0:rows, 0:1]),
            reads=[ckey], writes=[ckey], tag=f"ln_t1{tag}")
        p.op("vector", lambda v: v.tensor_sub(
            cols[0:rows, 2:3], cols[0:rows, 2:3], cols[0:rows, 3:4]),
            reads=[ckey], writes=[ckey], tag=f"ln_var{tag}")
        p.op("scalar", lambda s: s.activation(
            cols[0:rows, 3:4], cols[0:rows, 2:3], AF.Sqrt,
            bias=self.c["eps_col"][0][0:rows, :]),
            reads=[ckey, self.c["eps_col"][1]], writes=[ckey],
            tag=f"ln_sqrt{tag}")
        p.op("vector", lambda v: v.reciprocal(
            cols[0:rows, 1:2], cols[0:rows, 3:4]),
            reads=[ckey], writes=[ckey], tag=f"ln_recip{tag}")
        p.op("vector", lambda v: v.tensor_mul(
            cols[0:rows, 3:4], cols[0:rows, 0:1], cols[0:rows, 1:2]),
            reads=[ckey], writes=[ckey], tag=f"ln_nb0{tag}")
        p.op("vector", lambda v: v.tensor_scalar_mul(
            cols[0:rows, 3:4], cols[0:rows, 3:4], -1.0),
            reads=[ckey], writes=[ckey], tag=f"ln_nb1{tag}")
        p.op("scalar", lambda s: s.activation(
            out_ap, x_ap, AF.Identity, bias=cols[0:rows, 3:4],
            scale=cols[0:rows, 1:2]),
            reads=[xkey, ckey], writes=[outkey], tag=f"ln_out{tag}")
        a.release()


    def l0_attn(self):
        nc, p, a, hbm = self.nc, self.p, self.a, self.hbm
        SCALE = 0.125
        a.mark()

        qpk, kpk, vtok, ofm = [], [], [], []
        for c in range(ND):
            qpk.append(a.bf16(128, T, f"qpk{c}"))
            kpk.append(a.bf16(128, T, f"kpk{c}"))
        for tix in range(NT):
            vtok.append(a.bf16(128, 1024, f"vtok{tix}"))
        for c in range(ND):
            ofm.append(a.bf16(128, T, f"ofm{c}"))
        bq_cols, bq_ckey = self.load_cols("L0_attn_b_qkv", 3072)

        a.mark()
        p.dma("gpsimd", hbm["x0bf"], hbm["L0_attn_x"],
              reads=["L0_attn_x@hbm"], writes=["x0bf@hbm"], tag="x0 cast")
        xhat = []
        for c in range(ND):
            t, key = a.bf16(128, T, f"xhat{c}")
            p.dma("sync", t, hbm["x0bf"][:, c * 128:(c + 1) * 128],
                  reads=["x0bf@hbm"], writes=[key], transpose=True,
                  tag=f"xhat{c}")
            xhat.append((t, key))
        wsec = []
        for i in range(ND):
            wsec.append(a.bf16(128, 1024, f"wqkv_sec{i}"))
        vstage = [a.bf16(128, 512, f"vstage{k}") for k in range(2)]

        def load_wq_section(sec):
            for i in range(ND):
                t, key = wsec[i]
                p.dma("gpsimd", t,
                      hbm["L0_attn_w_qkv"][i * 128:(i + 1) * 128,
                                           sec * 1024:(sec + 1) * 1024],
                      reads=["L0_attn_w_qkv@hbm"], writes=[key],
                      tag=f"wqkv s{sec} i{i}")

        def qkv_chunk(oc, dest, dest_scale):
            so = oc % 8
            for tq in range(4):
                bank, bkey = self.banks.take()
                for i in range(ND):
                    p.op("tensor", lambda t, i=i, so=so, tq=tq, b=bank:
                         t.matmul(b, wsec[i][0][:, so * 128:(so + 1) * 128],
                                  xhat[i][0][:, tq * 512:(tq + 1) * 512],
                                  start=(i == 0), stop=(i == ND - 1)),
                         reads=[wsec[i][1], xhat[i][1]], writes=[bkey],
                         tag=f"qkvmm oc{oc} tq{tq} i{i}")
                if dest == "v":
                    vs, vskey = vstage[tq % 2]
                    p.op("scalar", lambda s, oc=oc, b=bank, vs=vs:
                         s.activation(vs, b, AF.Identity,
                                      bias=bq_cols[:, oc:oc + 1]),
                         reads=[bkey, bq_ckey], writes=[vskey],
                         tag=f"qkvev v oc{oc} tq{tq}")
                    p.dma("gpsimd",
                          hbm["vfm"][(oc - 16) * 128:(oc - 15) * 128,
                                     tq * 512:(tq + 1) * 512],
                          vs, reads=[vskey], writes=["vfm@hbm"],
                          tag=f"vfm st oc{oc} tq{tq}")
                else:
                    ap, key = dest
                    p.op("scalar", lambda s, oc=oc, tq=tq, b=bank, ap=ap,
                         sc=dest_scale:
                         s.activation(ap[:, tq * 512:(tq + 1) * 512], b,
                                      AF.Identity,
                                      bias=bq_cols[:, oc:oc + 1], scale=sc),
                         reads=[bkey, bq_ckey], writes=[key],
                         tag=f"qkvev oc{oc} tq{tq}")

        load_wq_section(2)
        for oc in range(16, 24):
            qkv_chunk(oc, "v", 1.0)
        for tix in range(NT):
            p.dma("sync", vtok[tix][0],
                  hbm["vfm"][:, tix * 128:(tix + 1) * 128],
                  reads=["vfm@hbm"], writes=[vtok[tix][1]], transpose=True,
                  tag=f"vtok{tix}")
        load_wq_section(0)
        for oc in range(0, 8):
            qkv_chunk(oc, qpk[oc], SCALE)
        load_wq_section(1)
        for oc in range(8, 16):
            qkv_chunk(oc, kpk[oc - 8], 1.0)
        a.release()

        a.mark()
        mx, mxkey = a.f32(128, 16, "amax")
        for c in range(ND):
            p.op("vector", lambda v, c=c: v.reduce_max(
                mx[:, c:c + 1], qpk[c][0][:, :], axis=AX.X,
                apply_absolute_value=True),
                reads=[qpk[c][1]], writes=[mxkey], tag=f"maxq{c}")
            p.op("vector", lambda v, c=c: v.reduce_max(
                mx[:, 8 + c:9 + c], kpk[c][0][:, :], axis=AX.X,
                apply_absolute_value=True),
                reads=[kpk[c][1]], writes=[mxkey], tag=f"maxk{c}")
        mx2, mx2key = a.f32(128, 2, "amax2")
        p.op("vector", lambda v: v.reduce_max(mx2[:, 0:1], mx[:, 0:8],
                                              axis=AX.X),
             reads=[mxkey], writes=[mx2key], tag="maxq")
        p.op("vector", lambda v: v.reduce_max(mx2[:, 1:2], mx[:, 8:16],
                                              axis=AX.X),
             reads=[mxkey], writes=[mx2key], tag="maxk")
        bank, bkey = self.banks.take()
        p.op("tensor", lambda t, b=bank: t.transpose(
            b[0:2, 0:128], mx2, self.c["identf"][0]),
            reads=[mx2key, self.c["identf"][1]], writes=[bkey], tag="maxT")
        mr, mrkey = a.f32(2, 130, "amaxr")
        p.op("vector", lambda v, b=bank: v.tensor_copy(mr[0:2, 0:128],
                                                       b[0:2, 0:128]),
             reads=[bkey], writes=[mrkey], tag="maxT_ev")
        p.op("vector", lambda v: v.reduce_max(mr[0:2, 128:129],
                                              mr[0:2, 0:128], axis=AX.X),
             reads=[mrkey], writes=[mrkey], tag="maxr")
        negm3, negmkey = a.f32(1, 3, "negM")
        negm = negm3[0:1, 0:1]
        bank2, b2key = self.banks.take()
        p.op("tensor", lambda t, b=bank2: t.transpose(
            b[0:1, 0:2], mr[0:2, 128:129], self.c["identf"][0][0:2, 0:2]),
            reads=[mrkey, self.c["identf"][1]], writes=[b2key], tag="negmT")
        p.op("vector", lambda v, b=bank2: v.tensor_copy(
            negm3[0:1, 1:3], b[0:1, 0:2]),
            reads=[b2key], writes=[negmkey], tag="negmTev")
        p.op("vector", lambda v: v.tensor_mul(
            negm3[0:1, 0:1], negm3[0:1, 1:2], negm3[0:1, 2:3]),
            reads=[negmkey], writes=[negmkey], tag="negm0")
        p.op("vector", lambda v: v.tensor_scalar_mul(
            negm3[0:1, 0:1], negm3[0:1, 0:1], -64.0),
            reads=[negmkey], writes=[negmkey], tag="negm1")

        qw, qwkey = a.bf16(128, T, "qwork")
        kw, kwkey = a.bf16(128, T, "kwork")
        p.op("vector", lambda v: v.memset(qw[64:128, :], 0.0),
             writes=[qwkey], tag="qw z")
        p.op("vector", lambda v: v.memset(kw[64:128, :], 0.0),
             writes=[kwkey], tag="kw z")
        p.op("scalar", lambda s: s.activation(
            qw[64:65, 0:T], self.c["ones_row_bf"][0], AF.Identity,
            scale=negm),
            reads=[self.c["ones_row_bf"][1], negmkey], writes=[qwkey],
            tag="qw stab")
        p.op("vector", lambda v: v.memset(kw[64:65, :], 1.0),
             writes=[kwkey], tag="kw stab")

        attnT = [a.bf16(128, 512, f"attnT{j}") for j in range(NT)]
        v65p = [[a.bf16(128, 65, f"v65_{k}_{j}") for j in range(NT)]
                for k in range(2)]
        for k in range(2):
            for j in range(NT):
                p.op("vector",
                     lambda v, k=k, j=j: v.memset(v65p[k][j][0][:, 64:65],
                                                  1.0),
                     writes=[v65p[k][j][1]], tag=f"v65 ones{k}{j}")
        srow, srkey = a.f32(1, T, "srow")
        rbc, rbckey = a.f32(64, T, "rbc")
        ou, oukey = a.f32(64, T, "ou")

        for h in range(16):
            ch, half = h // 2, h % 2
            p.op("vector", lambda v, ch=ch, half=half: v.tensor_copy(
                qw[0:64, :], qpk[ch][0][half * 64:half * 64 + 64, :]),
                reads=[qpk[ch][1]], writes=[qwkey], tag=f"qw cp h{h}")
            p.op("vector", lambda v, ch=ch, half=half: v.tensor_copy(
                kw[0:64, :], kpk[ch][0][half * 64:half * 64 + 64, :]),
                reads=[kpk[ch][1]], writes=[kwkey], tag=f"kw cp h{h}")
            v65 = v65p[h % 2]
            for j in range(NT):
                p.op("vector", lambda v, j=j, h=h, v65=v65: v.tensor_copy(
                    v65[j][0][:, 0:64],
                    vtok[j][0][:, h * 64:(h + 1) * 64]),
                    reads=[vtok[j][1]], writes=[v65[j][1]],
                    tag=f"v65 cp h{h} j{j}")
            for ib in range(4):
                for j in range(NT):
                    bS, bSkey = self.banks.take()
                    p.op("tensor", lambda t, j=j, ib=ib, b=bS: t.matmul(
                        b, kw[:, j * 128:(j + 1) * 128],
                        qw[:, ib * 512:(ib + 1) * 512],
                        start=True, stop=True),
                        reads=[qwkey, kwkey], writes=[bSkey],
                        tag=f"qk h{h} ib{ib} j{j}")
                    p.op("scalar", lambda s, j=j, b=bS: s.activation(
                        attnT[j][0], b, AF.Exp),
                        reads=[bSkey], writes=[attnT[j][1]],
                        tag=f"exp h{h} ib{ib} j{j}")
                bO, bOkey = self.banks.take()
                for j in range(NT):
                    p.op("tensor", lambda t, j=j, b=bO, v65=v65: t.matmul(
                        b[0:65, 0:512], v65[j][0], attnT[j][0],
                        start=(j == 0), stop=(j == NT - 1)),
                        reads=[v65[j][1], attnT[j][1]], writes=[bOkey],
                        tag=f"av h{h} ib{ib} j{j}")
                p.op("scalar", lambda s, ib=ib, b=bO: s.copy(
                    srow[0:1, ib * 512:(ib + 1) * 512], b[64:65, 0:512]),
                    reads=[bOkey], writes=[srkey], tag=f"srow h{h} ib{ib}")
                p.op("vector", lambda v, ib=ib, b=bO: v.tensor_copy(
                    ou[0:64, ib * 512:(ib + 1) * 512], b[0:64, 0:512]),
                    reads=[bOkey], writes=[oukey], tag=f"ou h{h} ib{ib}")
            p.op("vector", lambda v: v.reciprocal(srow, srow),
                 reads=[srkey], writes=[srkey], tag=f"rec h{h}")
            p.dma("gpsimd", hbm["rrow"][None, :], srow, reads=[srkey],
                  writes=["rrow@hbm"], tag=f"rrow st h{h}")
            p.dma("sync", rbc,
                  hbm["rrow"][None, :].partition_broadcast(64),
                  reads=["rrow@hbm"], writes=[rbckey], tag=f"rbc h{h}")
            p.op("vector", lambda v, h=h, half=half: v.tensor_tensor(
                ofm[h // 2][0][half * 64:half * 64 + 64, :],
                ou[0:64, :], rbc[0:64, :], ALU.mult),
                reads=[oukey, rbckey], writes=[ofm[h // 2][1]],
                tag=f"onorm h{h}")
        a.release()

        a.mark()
        wproj = []
        for cix in range(ND):
            t, key = a.bf16(128, 1024, f"wproj{cix}")
            p.dma("gpsimd", t,
                  hbm["L0_attn_w_proj"][cix * 128:(cix + 1) * 128, :],
                  reads=["L0_attn_w_proj@hbm"], writes=[key],
                  tag=f"wproj{cix}")
            wproj.append((t, key))
        bproj_bc, bpkey = self.bcast_row("L0_attn_b_proj", D)
        x0t = [a.f32(128, D, f"x0t{k}") for k in range(2)]
        xpre = [a.f32(128, D, f"xpre{k}") for k in range(2)]
        x1t = [a.f32(128, D, f"x1t{k}") for k in range(2)]
        for tc in range(NT):
            x0, x0key = x0t[tc % 2]
            xp, xpkey = xpre[tc % 2]
            x1, x1key = x1t[tc % 2]
            p.dma("sync", x0,
                  hbm["L0_attn_x"][tc * 128:(tc + 1) * 128, :],
                  reads=["L0_attn_x@hbm"], writes=[x0key], tag=f"x0 ld {tc}")
            for ob in range(2):
                bank, bkey = self.banks.take()
                for cix in range(ND):
                    p.op("tensor", lambda t, cix=cix, tc=tc, ob=ob, b=bank:
                         t.matmul(b, ofm[cix][0][:, tc * 128:(tc + 1) * 128],
                                  wproj[cix][0][:, ob * 512:(ob + 1) * 512],
                                  start=(cix == 0), stop=(cix == ND - 1)),
                         reads=[ofm[cix][1], wproj[cix][1]], writes=[bkey],
                         tag=f"proj tc{tc} ob{ob} c{cix}")
                p.op("vector", lambda v, ob=ob, b=bank, xp=xp: v.tensor_tensor(
                    xp[:, ob * 512:(ob + 1) * 512], b,
                    bproj_bc[:, ob * 512:(ob + 1) * 512], ALU.add),
                    reads=[bkey, bpkey], writes=[xpkey],
                    tag=f"projev tc{tc} ob{ob}")
            p.op("vector", lambda v, xp=xp, x0=x0: v.tensor_add(xp, xp, x0),
                 reads=[xpkey, x0key], writes=[xpkey], tag=f"res tc{tc}")
            self.ln_tile(xp, xpkey, x1, x1key, tag=f"l0a{tc}")
            p.dma("sync", hbm["x1"][tc * 128:(tc + 1) * 128, :], x1,
                  reads=[x1key], writes=["x1@hbm"], tag=f"x1 st {tc}")
            p.dma("gpsimd", hbm["x1bf"][tc * 128:(tc + 1) * 128, :], x1,
                  reads=[x1key], writes=["x1bf@hbm"], tag=f"x1bf st {tc}")
        a.release()
        a.release()



    def _prep_weight(self, src, dst, O, I, scols):
        p, a, hbm = self.p, self.a, self.hbm
        a.mark()
        f32b = [a.f32(128, I, f"wprep_f{k}") for k in range(3)]
        bfb = [a.bf16(128, I, f"wprep_b{k}") for k in range(3)]
        sc, sckey = scols
        for oc in range(O // 128):
            fap, fkey = f32b[oc % 3]
            bap, bkey = bfb[oc % 3]
            p.dma("scalar", fap, hbm[src][oc * 128:(oc + 1) * 128, :],
                  reads=[src + "@hbm"], writes=[fkey], tag=f"wp ld {src} {oc}")
            p.op("vector", lambda v, fap=fap, bap=bap, oc=oc:
                 v.tensor_scalar_mul(bap, fap, sc[:, oc:oc + 1]),
                 reads=[fkey, sckey], writes=[bkey], tag=f"wp sc {src} {oc}")
            p.dma("gpsimd", hbm[dst][oc * 128:(oc + 1) * 128, :], bap,
                  reads=[bkey], writes=[f"{dst}@hbm:{oc // 4}"],
                  tag=f"wp st {src} {oc}")
        a.release()

    def l0_ffn(self):
        p, a, hbm = self.p, self.a, self.hbm
        H = 4096
        a.mark()
        s1c = self.load_cols("L0_ffn_w1_s", H)
        s2c = self.load_cols("L0_ffn_w2_s", H)
        h1c = self.load_cols("L0_ffn_h1_s", H)
        h2c = self.load_cols("L0_ffn_h2_s", H)
        w3c = self.load_cols("L0_ffn_w3_s", D)

        for ts in range(2):
            r0 = ts * 1024
            if ts == 0:
                self._prep_weight("L0_ffn_w1_w", "w1bf", H, D, s1c)
                self._prep_weight("L0_ffn_w2_w", "w2bf", H, D, s2c)

            a.mark()
            xh = []
            for c in range(ND):
                t, key = a.bf16(128, 1024, f"xh{c}")
                p.dma("sync", t, hbm["x1bf"][r0:r0 + 1024,
                                             c * 128:(c + 1) * 128],
                      reads=["x1bf@hbm"], writes=[key], transpose=True,
                      tag=f"xh{c} ts{ts}")
                xh.append((t, key))
            w1w = [a.bf16(128, 8 * 512, f"w1w{k}") for k in range(3)]
            w2w = [a.bf16(128, 8 * 512, f"w2w{k}") for k in range(3)]
            hsil = [a.bf16(128, 512, f"hsil{k}") for k in range(2)]
            hst = [a.bf16(128, 512, f"hst{k}") for k in range(2)]

            def s1_load(ob):
                par = ob % 3
                for i in range(ND):
                    for (win, img) in ((w1w, "w1bf"), (w2w, "w2bf")):
                        t, key = win[par]
                        p.dma("sync", t[:, i * 512:(i + 1) * 512],
                              hbm[img][ob * 512:(ob + 1) * 512,
                                       i * 128:(i + 1) * 128],
                              reads=[f"{img}@hbm:{ob}"], writes=[key],
                              transpose=True, tag=f"{img}w ob{ob} i{i}")

            def s1_compute(ob):
                par = ob % 3
                for tc in range(8):
                    bA, bAkey = self.banks.take()
                    bB, bBkey = self.banks.take()
                    for i in range(ND):
                        p.op("tensor", lambda t, i=i, tc=tc, par=par, b=bA:
                             t.matmul(b, xh[i][0][:, tc * 128:(tc + 1) * 128],
                                      w1w[par][0][:, i * 512:(i + 1) * 512],
                                      start=(i == 0), stop=(i == ND - 1)),
                             reads=[xh[i][1], w1w[par][1]], writes=[bAkey],
                             tag=f"s1A ts{ts} ob{ob} tc{tc} i{i}")
                        p.op("tensor", lambda t, i=i, tc=tc, par=par, b=bB:
                             t.matmul(b, xh[i][0][:, tc * 128:(tc + 1) * 128],
                                      w2w[par][0][:, i * 512:(i + 1) * 512],
                                      start=(i == 0), stop=(i == ND - 1)),
                             reads=[xh[i][1], w2w[par][1]], writes=[bBkey],
                             tag=f"s1B ts{ts} ob{ob} tc{tc} i{i}")
                    sil, silkey = hsil[tc % 2]
                    hs, hskey = hst[tc % 2]
                    p.op("scalar", lambda s, b=bA, sil=sil:
                         s.activation(sil, b, AF.Silu),
                         reads=[bAkey], writes=[silkey],
                         tag=f"s1 sil ts{ts} ob{ob} tc{tc}")
                    p.op("vector", lambda v, b=bB, sil=sil, hs=hs:
                         v.tensor_tensor(hs, b, sil, ALU.mult),
                         reads=[bBkey, silkey], writes=[hskey],
                         tag=f"s1 mul ts{ts} ob{ob} tc{tc}")
                    p.dma("gpsimd",
                          hbm["h"][r0 + tc * 128:r0 + (tc + 1) * 128,
                                   ob * 512:(ob + 1) * 512], hs,
                          reads=[hskey], writes=["h@hbm"],
                          tag=f"h st ts{ts} ob{ob} tc{tc}")

            s1_load(0)
            s1_load(1)
            for ob in range(8):
                if ob + 2 < 8:
                    s1_load(ob + 2)
                s1_compute(ob)
            a.release()
            if ts == 0:
                self._prep_weight("L0_ffn_h1_w", "h1bf", H, H, h1c)

            for (src_t, img, dst) in (("h", "h1bf", "g1"),
                                      ("g1", "h2bf", "g2")):
                a.mark()
                hf = []
                for ic in range(32):
                    t, key = a.bf16(128, 1024, f"hf{ic}")
                    p.dma("sync", t, hbm[src_t][r0:r0 + 1024,
                                                ic * 128:(ic + 1) * 128],
                          reads=[src_t + "@hbm"], writes=[key], transpose=True,
                          tag=f"{src_t}f{ic} ts{ts}")
                    hf.append((t, key))
                win = [a.bf16(128, 32 * 512, f"hw{k}") for k in range(3)]
                gst = [a.bf16(128, 512, f"gst{k}") for k in range(2)]

                def h_load(ob):
                    t, key = win[ob % 3]
                    for ic in range(32):
                        p.dma("sync", t[:, ic * 512:(ic + 1) * 512],
                              hbm[img][ob * 512:(ob + 1) * 512,
                                       ic * 128:(ic + 1) * 128],
                              reads=[f"{img}@hbm:{ob}"], writes=[key],
                              transpose=True, tag=f"{img}w ob{ob} ic{ic}")

                def h_compute(ob):
                    t, key = win[ob % 3]
                    for tc in range(8):
                        bA, bAkey = self.banks.take()
                        for ic in range(32):
                            p.op("tensor",
                                 lambda tt, ic=ic, tc=tc, t=t, b=bA:
                                 tt.matmul(
                                     b, hf[ic][0][:, tc * 128:(tc + 1) * 128],
                                     t[:, ic * 512:(ic + 1) * 512],
                                     start=(ic == 0), stop=(ic == 31)),
                                 reads=[hf[ic][1], key], writes=[bAkey],
                                 tag=f"{img} ts{ts} ob{ob} tc{tc} ic{ic}")
                        g, gkey = gst[tc % 2]
                        p.op("scalar", lambda s, b=bA, g=g:
                             s.activation(g, b, AF.Gelu),
                             reads=[bAkey], writes=[gkey],
                             tag=f"{img} g ts{ts} ob{ob} tc{tc}")
                        p.dma("gpsimd",
                              hbm[dst][r0 + tc * 128:r0 + (tc + 1) * 128,
                                       ob * 512:(ob + 1) * 512], g,
                              reads=[gkey], writes=[dst + "@hbm"],
                              tag=f"{dst} st ts{ts} ob{ob} tc{tc}")

                h_load(0)
                h_load(1)
                for ob in range(8):
                    if ob + 2 < 8:
                        h_load(ob + 2)
                    h_compute(ob)
                a.release()
                if ts == 0 and src_t == "h":
                    self._prep_weight("L0_ffn_h2_w", "h2bf", H, H, h2c)
                if ts == 0 and src_t == "g1":
                    self._prep_weight("L0_ffn_w3_w", "w3bf", D, H, w3c)

            a.mark()
            gf = []
            for ic in range(32):
                t, key = a.bf16(128, 1024, f"gf{ic}")
                p.dma("sync", t, hbm["g2"][r0:r0 + 1024,
                                           ic * 128:(ic + 1) * 128],
                      reads=["g2@hbm"], writes=[key], transpose=True,
                      tag=f"g2f{ic} ts{ts}")
                gf.append((t, key))
            w3w, w3wkey = a.bf16(128, 32 * 1024, "w3w")
            for ic in range(32):
                p.dma("sync", w3w[:, ic * 1024:(ic + 1) * 1024],
                      hbm["w3bf"][0:1024, ic * 128:(ic + 1) * 128],
                      reads=["w3bf@hbm:0", "w3bf@hbm:1"], writes=[w3wkey],
                      transpose=True, tag=f"w3w{ic} ts{ts}")
            x1t = [a.f32(128, D, f"fx1{k}") for k in range(2)]
            xpre = [a.f32(128, D, f"fxp{k}") for k in range(2)]
            x2t = [a.f32(128, D, f"fx2{k}") for k in range(2)]
            for tc in range(8):
                x1, x1key = x1t[tc % 2]
                xp, xpkey = xpre[tc % 2]
                x2, x2key = x2t[tc % 2]
                p.dma("sync", x1, hbm["x1"][r0 + tc * 128:r0 + (tc + 1) * 128, :],
                      reads=["x1@hbm"], writes=[x1key], tag=f"x1 ld {ts}{tc}")
                bA, bAkey = self.banks.take()
                bB, bBkey = self.banks.take()
                for ic in range(32):
                    p.op("tensor", lambda t, ic=ic, tc=tc, b=bA:
                         t.matmul(b, gf[ic][0][:, tc * 128:(tc + 1) * 128],
                                  w3w[:, ic * 1024:ic * 1024 + 512],
                                  start=(ic == 0), stop=(ic == 31)),
                         reads=[gf[ic][1], w3wkey], writes=[bAkey],
                         tag=f"w3A ts{ts} tc{tc} ic{ic}")
                    p.op("tensor", lambda t, ic=ic, tc=tc, b=bB:
                         t.matmul(b, gf[ic][0][:, tc * 128:(tc + 1) * 128],
                                  w3w[:, ic * 1024 + 512:(ic + 1) * 1024],
                                  start=(ic == 0), stop=(ic == 31)),
                         reads=[gf[ic][1], w3wkey], writes=[bBkey],
                         tag=f"w3B ts{ts} tc{tc} ic{ic}")
                p.op("vector", lambda v, b=bA, xp=xp, x1=x1:
                     v.tensor_tensor(xp[:, 0:512], b, x1[:, 0:512], ALU.add),
                     reads=[bAkey, x1key], writes=[xpkey],
                     tag=f"w3evA ts{ts} tc{tc}")
                p.op("vector", lambda v, b=bB, xp=xp, x1=x1:
                     v.tensor_tensor(xp[:, 512:1024], b, x1[:, 512:1024],
                                     ALU.add),
                     reads=[bBkey, x1key], writes=[xpkey],
                     tag=f"w3evB ts{ts} tc{tc}")
                self.ln_tile(xp, xpkey, x2, x2key, tag=f"l0f{ts}{tc}")
                p.dma("sync", hbm["x2"][r0 + tc * 128:r0 + (tc + 1) * 128, :],
                      x2, reads=[x2key], writes=["x2@hbm"],
                      tag=f"x2 st {ts}{tc}")
                p.dma("gpsimd", hbm["x2bf"][r0 + tc * 128:r0 + (tc + 1) * 128, :],
                      x2, reads=[x2key], writes=["x2bf@hbm"],
                      tag=f"x2bf st {ts}{tc}")
            a.release()
        a.release()


    def l1_attn(self):
        p, a, hbm = self.p, self.a, self.hbm
        a.mark()

        vtok = [a.bf16(128, 2048, f"avtok{t}") for t in range(NT)]
        qt, qtkey = a.bf16(128, T, "aqt")
        kt, ktkey = a.bf16(128, T, "akt")

        a.mark()
        g_bc, gbkey = self.bcast_row("L1_attn_ln_scale", D)
        b_bc, bbkey = self.bcast_row("L1_attn_ln_bias", D)
        x2t = [a.f32(128, D, f"ax2{k}") for k in range(2)]
        xnt = [a.f32(128, D, f"axn{k}") for k in range(2)]
        for tc in range(NT):
            x2, x2key = x2t[tc % 2]
            xn, xnkey = xnt[tc % 2]
            p.dma("sync", x2, hbm["x2"][tc * 128:(tc + 1) * 128, :],
                  reads=["x2@hbm"], writes=[x2key], tag=f"ax2 ld {tc}")
            self.ln_tile(x2, x2key, xn, xnkey, tag=f"l1a{tc}")
            p.op("vector", lambda v, xn=xn: v.tensor_mul(xn, xn, g_bc),
                 reads=[xnkey, gbkey], writes=[xnkey], tag=f"xng {tc}")
            p.op("vector", lambda v, xn=xn: v.tensor_add(xn, xn, b_bc),
                 reads=[xnkey, bbkey], writes=[xnkey], tag=f"xnb {tc}")
            p.dma("gpsimd", hbm["xnbf"][tc * 128:(tc + 1) * 128, :], xn,
                  reads=[xnkey], writes=["xnbf@hbm"], tag=f"xnbf st {tc}")
        a.release()

        a.mark()
        xnf = []
        for c in range(ND):
            t, key = a.bf16(128, T, f"axnf{c}")
            p.dma("sync", t, hbm["xnbf"][:, c * 128:(c + 1) * 128],
                  reads=["xnbf@hbm"], writes=[key], transpose=True,
                  tag=f"axnf{c}")
            xnf.append((t, key))
        bh_bc, bhkey = a.f32(128, 2048, "bh_bc")
        whid = [a.bf16(128, 2048, f"awh{d}") for d in range(ND)]
        hstg = [a.f32(128, 512, f"ahstg{k}") for k in range(2)]
        gstg = [a.bf16(128, 512, f"agstg{k}") for k in range(2)]

        for half in range(2):
            p.dma("sync", bh_bc,
                  self.hbm["L1_attn_b_hidden"][half * 2048:(half + 1) * 2048]
                  [None, :].partition_broadcast(128),
                  reads=["L1_attn_b_hidden@hbm"], writes=[bhkey],
                  tag=f"bh_bc h{half}")
            for d in range(ND):
                t, key = whid[d]
                p.dma("gpsimd", t,
                      hbm["L1_attn_W_hidden"][d * 128:(d + 1) * 128,
                                              half * 2048:(half + 1) * 2048],
                      reads=["L1_attn_W_hidden@hbm"], writes=[key],
                      tag=f"awh{d} h{half}")
            for tc in range(NT):
                for ob in range(4):
                    gob = half * 4 + ob
                    bank, bkey = self.banks.take()
                    for d in range(ND):
                        p.op("tensor", lambda t, d=d, tc=tc, ob=ob, b=bank:
                             t.matmul(b, xnf[d][0][:, tc * 128:(tc + 1) * 128],
                                      whid[d][0][:, ob * 512:(ob + 1) * 512],
                                      start=(d == 0), stop=(d == ND - 1)),
                             reads=[xnf[d][1], whid[d][1]], writes=[bkey],
                             tag=f"hid tc{tc} ob{gob} d{d}")
                    hs, hskey = hstg[tc % 2]
                    p.op("vector", lambda v, b=bank, hs=hs, ob=ob:
                         v.tensor_tensor(hs, b,
                                         bh_bc[:, ob * 512:(ob + 1) * 512],
                                         ALU.add),
                         reads=[bkey, bhkey], writes=[hskey],
                         tag=f"hidb tc{tc} ob{ob}")
                    if half == 0:
                        p.op("scalar", lambda s, hs=hs, tc=tc, ob=ob:
                             s.activation(vtok[tc][0][:, ob * 512:
                                                      (ob + 1) * 512],
                                          hs, AF.Silu),
                             reads=[hskey], writes=[vtok[tc][1]],
                             tag=f"hidv tc{tc} ob{ob}")
                    else:
                        gs, gskey = gstg[tc % 2]
                        p.op("scalar", lambda s, hs=hs, gs=gs:
                             s.activation(gs, hs, AF.Silu),
                             reads=[hskey], writes=[gskey],
                             tag=f"hidg tc{tc} ob{ob}")
                        p.dma("gpsimd",
                              hbm["gate"][tc * 128:(tc + 1) * 128,
                                          ob * 512:(ob + 1) * 512], gs,
                              reads=[gskey], writes=["gate@hbm"],
                              tag=f"gate st tc{tc} ob{ob}")

        bqk_c, bqkkey = self.load_cols("L1_attn_b_qk", 128)
        w0c, w0key = self.load_cols("L1_attn_os_weight", 128, slot=0)
        w1c, w1key = self.load_cols("L1_attn_os_weight", 128, slot=1)
        b0c, b0key = self.load_cols("L1_attn_os_bias", 128, slot=0)
        b1c, b1key = self.load_cols("L1_attn_os_bias", 128, slot=1)
        p.op("vector", lambda v: v.tensor_scalar_mul(w0c, w0c, 1.0 / T),
             reads=[w0key], writes=[w0key], tag="w0n")
        p.op("vector", lambda v: v.tensor_scalar_mul(b0c, b0c, 1.0 / T),
             reads=[b0key], writes=[b0key], tag="b0n")
        wqk = []
        for d in range(ND):
            t, key = a.bf16(128, 128, f"awqk{d}")
            p.dma("gpsimd", t, hbm["L1_attn_W_qk"][d * 128:(d + 1) * 128, :],
                  reads=["L1_attn_W_qk@hbm"], writes=[key], tag=f"awqk{d}")
            wqk.append((t, key))
        qkstg, qkskey = a.f32(128, 512, "aqkstg")
        for tb in range(4):
            bank, bkey = self.banks.take()
            for d in range(ND):
                p.op("tensor", lambda t, d=d, tb=tb, b=bank:
                     t.matmul(b, wqk[d][0],
                              xnf[d][0][:, tb * 512:(tb + 1) * 512],
                              start=(d == 0), stop=(d == ND - 1)),
                     reads=[wqk[d][1], xnf[d][1]], writes=[bkey],
                     tag=f"qk tb{tb} d{d}")
            p.op("scalar", lambda s, b=bank: s.activation(
                qkstg, b, AF.Silu, bias=bqk_c),
                reads=[bkey, bqkkey], writes=[qkskey], tag=f"qksilu {tb}")
            p.op("scalar", lambda s, tb=tb: s.activation(
                qt[:, tb * 512:(tb + 1) * 512], qkstg, AF.Identity,
                bias=b0c, scale=w0c),
                reads=[qkskey, b0key, w0key], writes=[qtkey],
                tag=f"qt {tb}")
            p.op("scalar", lambda s, tb=tb: s.activation(
                kt[:, tb * 512:(tb + 1) * 512], qkstg, AF.Identity,
                bias=b1c, scale=w1c),
                reads=[qkskey, b1key, w1key], writes=[ktkey],
                tag=f"kt {tb}")
        a.release()

        a.mark()
        attnT = [a.bf16(128, 512, f"aat{j}") for j in range(NT)]
        rstg = [a.f32(128, 512, f"arstg{k}") for k in range(2)]
        gfm = [a.bf16(128, T, f"agfm{k}") for k in range(2)]
        ogm = [a.bf16(128, 1024, f"aog{c}") for c in range(16)]
        wout = []
        for cix in range(16):
            wout.append(a.bf16(128, 1024, f"awo{cix}"))
        bout_bc, bokey = self.bcast_row("L1_attn_b_out", D)
        x2t2 = [a.f32(128, D, f"bx2{k}") for k in range(2)]
        xpre = [a.f32(128, D, f"bxp{k}") for k in range(2)]
        x3t = [a.f32(128, D, f"bx3{k}") for k in range(2)]
        for cix in range(16):
            t, key = wout[cix]
            p.dma("gpsimd", t,
                  hbm["L1_attn_W_out"][cix * 128:(cix + 1) * 128, :],
                  reads=["L1_attn_W_out@hbm"], writes=[key], tag=f"awo{cix}")

        for ih in range(2):
            for ib2 in range(2):
                ib = ih * 2 + ib2
                for j in range(NT):
                    bank, bkey = self.banks.take()
                    p.op("tensor", lambda t, j=j, ib=ib, b=bank: t.matmul(
                        b, kt[:, j * 128:(j + 1) * 128],
                        qt[:, ib * 512:(ib + 1) * 512], start=True, stop=True),
                        reads=[qtkey, ktkey], writes=[bkey],
                        tag=f"sim ib{ib} j{j}")
                    rs, rskey = rstg[j % 2]
                    p.op("vector", lambda v, b=bank, rs=rs:
                         v.tensor_single_scalar(rs, b, 0.0, ALU.max),
                         reads=[bkey], writes=[rskey],
                         tag=f"relu ib{ib} j{j}")
                    p.op("scalar", lambda s, j=j, rs=rs: s.activation(
                        attnT[j][0], rs, AF.Square),
                        reads=[rskey], writes=[attnT[j][1]],
                        tag=f"sq ib{ib} j{j}")
                for cix in range(16):
                    gf, gfkey = gfm[cix % 2]
                    p.dma("sync", gf,
                          hbm["gate"][:, cix * 128:(cix + 1) * 128],
                          reads=["gate@hbm"], writes=[gfkey], transpose=True,
                          tag=f"gfm ib{ib} c{cix}")
                    bank, bkey = self.banks.take()
                    for j in range(NT):
                        p.op("tensor", lambda t, j=j, cix=cix, b=bank:
                             t.matmul(
                                 b, vtok[j][0][:, cix * 128:(cix + 1) * 128],
                                 attnT[j][0], start=(j == 0),
                                 stop=(j == NT - 1)),
                             reads=[vtok[j][1], attnT[j][1]], writes=[bkey],
                             tag=f"av ib{ib} c{cix} j{j}")
                    p.op("vector", lambda v, b=bank, cix=cix, ib=ib, ib2=ib2,
                         gf=gf:
                         v.tensor_tensor(
                             ogm[cix][0][:, ib2 * 512:(ib2 + 1) * 512],
                             b, gf[:, ib * 512:(ib + 1) * 512], ALU.mult),
                         reads=[bkey, gfkey], writes=[ogm[cix][1]],
                         tag=f"og ib{ib} c{cix}")
            for tc8 in range(8):
                tc = ih * 8 + tc8
                x2, x2key = x2t2[tc % 2]
                xp, xpkey = xpre[tc % 2]
                x3, x3key = x3t[tc % 2]
                p.dma("sync", x2, hbm["x2"][tc * 128:(tc + 1) * 128, :],
                      reads=["x2@hbm"], writes=[x2key], tag=f"bx2 ld {tc}")
                for ob in range(2):
                    bank, bkey = self.banks.take()
                    for cix in range(16):
                        p.op("tensor",
                             lambda t, cix=cix, tc8=tc8, ob=ob, b=bank:
                             t.matmul(
                                 b,
                                 ogm[cix][0][:, tc8 * 128:(tc8 + 1) * 128],
                                 wout[cix][0][:, ob * 512:(ob + 1) * 512],
                                 start=(cix == 0), stop=(cix == 15)),
                             reads=[ogm[cix][1], wout[cix][1]],
                             writes=[bkey], tag=f"wo tc{tc} ob{ob} c{cix}")
                    p.op("vector", lambda v, b=bank, xp=xp, ob=ob:
                         v.tensor_tensor(
                             xp[:, ob * 512:(ob + 1) * 512], b,
                             bout_bc[:, ob * 512:(ob + 1) * 512], ALU.add),
                         reads=[bkey, bokey], writes=[xpkey],
                         tag=f"woev tc{tc} ob{ob}")
                p.op("vector", lambda v, xp=xp, x2=x2:
                     v.tensor_add(xp, xp, x2),
                     reads=[xpkey, x2key], writes=[xpkey], tag=f"bres {tc}")
                self.ln_tile(xp, xpkey, x3, x3key, tag=f"l1o{tc}")
                p.dma("sync", hbm["x3"][tc * 128:(tc + 1) * 128, :], x3,
                      reads=[x3key], writes=["x3@hbm"], tag=f"x3 st {tc}")
                p.dma("gpsimd", hbm["x3bf"][tc * 128:(tc + 1) * 128, :], x3,
                      reads=[x3key], writes=["x3bf@hbm"],
                      tag=f"x3bf st {tc}")
        a.release()
        a.release()


    def sb_row_to_cols(self, row, rowkey, n, name, rows=128):
        a, p = self.a, self.p
        nch = (n + 127) // 128
        cols, ckey = a.f32(128, nch, name)
        identf1 = self.c["identf1"]
        for cix in range(nch):
            w = min(128, n - cix * 128)
            bank, bkey = self.banks.take()
            p.op("tensor", lambda t, c=cix, b=bank, w=w, row=row: t.transpose(
                b[0:w, 0:1], row[0:1, c * 128:c * 128 + w], identf1[0]),
                reads=[rowkey, identf1[1]], writes=[bkey],
                tag=f"sr2c:{name}:{cix}")
            p.op("vector", lambda v, c=cix, b=bank, w=w: v.tensor_copy(
                cols[0:w, c:c + 1], b[0:w, 0:1]),
                reads=[bkey], writes=[ckey], tag=f"sr2c_ev:{name}:{cix}")
        return cols, ckey

    def l1_ffn(self):
        p, a, hbm = self.p, self.a, self.hbm
        NS = 64
        a.mark()

        Ssl, Sslkey = a.f32(64, D, "Sslot")
        p.dma("sync", Ssl, hbm["L1_ffn_slot_init"],
              reads=["L1_ffn_slot_init@hbm"], writes=[Sslkey], tag="S ld")
        Sfm, Sfmkey = a.f32(128, ND * 64, "Sfm")
        Sfb, Sfbkey = a.bf16(128, ND * 64, "Sfb")
        smb, smbkey = a.bf16(128, ND, "smb")
        smf, smfkey = a.f32(128, ND, "smf")
        for c in range(ND):
            bank, bkey = self.banks.take()
            p.op("tensor", lambda t, c=c, b=bank: t.transpose(
                b[0:128, 0:64], Ssl[0:64, c * 128:(c + 1) * 128],
                self.c["identf"][0][0:64, 0:64]),
                reads=[Sslkey, self.c["identf"][1]], writes=[bkey],
                tag=f"ST{c}")
            p.op("vector", lambda v, c=c, b=bank: v.tensor_copy(
                Sfm[:, c * 64:(c + 1) * 64], b[0:128, 0:64]),
                reads=[bkey], writes=[Sfmkey], tag=f"STev{c}")
            p.op("vector", lambda v, c=c, b=bank: v.tensor_copy(
                Sfb[:, c * 64:(c + 1) * 64], b[0:128, 0:64]),
                reads=[bkey], writes=[Sfbkey], tag=f"STevb{c}")
            p.op("vector", lambda v, c=c: v.tensor_scalar(
                smf[:, c:c + 1], Sfm[:, c * 64:(c + 1) * 64], 1.0, 0.0,
                ALU.mult, ALU.add, accum_out=smf[:, c:c + 1])
                if False else v.reduce_sum(
                smf[:, c:c + 1], Sfm[:, c * 64:(c + 1) * 64], axis=AX.X),
                reads=[Sfmkey], writes=[smfkey], tag=f"sm{c}")
        p.op("vector", lambda v: v.tensor_scalar_mul(smf, smf, 1.0 / NS),
             reads=[smfkey], writes=[smfkey], tag="sm scale")
        p.op("vector", lambda v: v.tensor_copy(smb, smf),
             reads=[smfkey], writes=[smbkey], tag="sm cast")

        alphat = [None] * NT
        alphaT, alTkey = a.bf16(128, T, "alphaT")
        p.op("vector", lambda v: v.memset(alphaT[64:128, :], 0.0),
             writes=[alTkey], tag="alphaT z")
        x3b = [a.bf16(128, D, f"fx3b{t}") for t in range(NT)]
        for t in range(NT):
            p.dma("sync", x3b[t][0], hbm["x3bf"][t * 128:(t + 1) * 128, :],
                  reads=["x3bf@hbm"], writes=[x3b[t][1]], tag=f"x3b {t}")

        a.mark()
        w1s = []
        for i in range(16):
            t, key = a.bf16(128, 512, f"fw1{i}")
            p.dma("gpsimd", t, hbm["L1_ffn_w1"][i * 128:(i + 1) * 128, :],
                  reads=["L1_ffn_w1@hbm"], writes=[key], tag=f"fw1{i}")
            w1s.append((t, key))
        bank, bkey = self.banks.take()
        for c in range(ND):
            p.op("tensor", lambda t, c=c, b=bank: t.matmul(
                b[0:1, 0:512], smb[:, c:c + 1], w1s[8 + c][0],
                start=(c == 0), stop=(c == ND - 1)),
                reads=[smbkey, w1s[8 + c][1]], writes=[bkey], tag=f"cst{c}")
        b1row, b1key = a.f32(1, 512, "b1row")
        p.dma("sync", b1row, hbm["L1_ffn_b1"][None, :],
              reads=["L1_ffn_b1@hbm"], writes=[b1key], tag="b1 ld")
        cstrow, cstrkey = a.f32(1, 512, "cstrow")
        p.op("vector", lambda v, b=bank: v.tensor_tensor(
            cstrow, b[0:1, 0:512], b1row, ALU.add),
            reads=[bkey, b1key], writes=[cstrkey], tag="cst+b1")
        cstc, cstckey = self.sb_row_to_cols(cstrow, cstrkey, 512, "cstc")

        xf3 = []
        for c in range(ND):
            t, key = a.bf16(128, T, f"fxf{c}")
            p.dma("sync", t, hbm["x3bf"][:, c * 128:(c + 1) * 128],
                  reads=["x3bf@hbm"], writes=[key], transpose=True,
                  tag=f"fxf{c}")
            xf3.append((t, key))
        hfm, hfmkey = a.bf16(128, 4 * T, "fhfm")
        for oc in range(4):
            for tb in range(4):
                bank, bkey = self.banks.take()
                for i in range(ND):
                    p.op("tensor", lambda t, i=i, oc=oc, tb=tb, b=bank:
                         t.matmul(b, w1s[i][0][:, oc * 128:(oc + 1) * 128],
                                  xf3[i][0][:, tb * 512:(tb + 1) * 512],
                                  start=(i == 0), stop=(i == ND - 1)),
                         reads=[w1s[i][1], xf3[i][1]], writes=[bkey],
                         tag=f"fh oc{oc} tb{tb} i{i}")
                p.op("scalar", lambda s, oc=oc, tb=tb, b=bank: s.activation(
                    hfm[:, oc * T + tb * 512:oc * T + (tb + 1) * 512], b,
                    AF.Gelu, bias=cstc[:, oc:oc + 1]),
                    reads=[bkey, cstckey], writes=[hfmkey],
                    tag=f"fhev oc{oc} tb{tb}")
        w2s = []
        for hc in range(4):
            t, key = a.bf16(128, 64, f"fw2{hc}")
            p.dma("gpsimd", t, hbm["L1_ffn_w2"][hc * 128:(hc + 1) * 128, :],
                  reads=["L1_ffn_w2@hbm"], writes=[key], tag=f"fw2{hc}")
            w2s.append((t, key))
        b2c, b2ckey = a.f32(64, 1, "b2c")
        p.dma("sync", b2c, hbm["L1_ffn_b2"].rearrange("(p one) -> p one",
                                                      one=1),
              reads=["L1_ffn_b2@hbm"], writes=[b2ckey], tag="b2 ld")
        lgf, lgfkey = a.f32(64, T, "lgfm")
        for tb in range(4):
            bank, bkey = self.banks.take()
            for hc in range(4):
                p.op("tensor", lambda t, hc=hc, tb=tb, b=bank: t.matmul(
                    b[0:64, 0:512], w2s[hc][0],
                    hfm[:, hc * T + tb * 512:hc * T + (tb + 1) * 512],
                    start=(hc == 0), stop=(hc == 3)),
                    reads=[w2s[hc][1], hfmkey], writes=[bkey],
                    tag=f"lg tb{tb} hc{hc}")
            p.op("scalar", lambda s, tb=tb, b=bank: s.activation(
                lgf[:, tb * 512:(tb + 1) * 512], b[0:64, 0:512], AF.Identity,
                bias=b2c),
                reads=[bkey, b2ckey], writes=[lgfkey], tag=f"lgev tb{tb}")
        tau1, tau1key = a.f32(1, 2, "tau1")
        p.dma("sync", tau1[0:1, 0:1],
              hbm["L1_ffn_tau"][None, :],
              reads=["L1_ffn_tau@hbm"], writes=[tau1key], tag="tau ld")
        p.op("scalar", lambda s: s.activation(tau1[0:1, 1:2], tau1[0:1, 0:1],
                                              AF.Abs),
             reads=[tau1key], writes=[tau1key], tag="tau abs")
        p.op("vector", lambda v: v.tensor_scalar_add(tau1[0:1, 1:2],
                                                     tau1[0:1, 1:2], 0.1),
             reads=[tau1key], writes=[tau1key], tag="tau add")
        p.op("vector", lambda v: v.reciprocal(tau1[0:1, 0:1], tau1[0:1, 1:2]),
             reads=[tau1key], writes=[tau1key], tag="tau rec")
        p.dma("gpsimd", hbm["tauw"][None, :], tau1[0:1, 0:1],
              reads=[tau1key], writes=["tauw@hbm"], tag="tau st")
        tinv, tinvkey = a.f32(128, 1, "tinv")
        p.dma("sync", tinv, hbm["tauw"][None, :].partition_broadcast(128),
              reads=["tauw@hbm"], writes=[tinvkey], tag="tau bc")

        ltok, ltokey = a.f32(128, NT * 64, "ltok")
        for tix in range(NT):
            bank, bkey = self.banks.take()
            p.op("tensor", lambda t, tix=tix, b=bank: t.transpose(
                b[0:128, 0:64], lgf[0:64, tix * 128:(tix + 1) * 128],
                self.c["identf"][0][0:64, 0:64]),
                reads=[lgfkey, self.c["identf"][1]], writes=[bkey],
                tag=f"lT{tix}")
            p.op("vector", lambda v, tix=tix, b=bank: v.tensor_copy(
                ltok[:, tix * 64:(tix + 1) * 64], b[0:128, 0:64]),
                reads=[bkey], writes=[ltokey], tag=f"lTev{tix}")
        wk3t, wkkey = a.f32(128, NT * 64, "tkwork")
        eqa, eqkey = a.f32(128, NT * 64, "tkeq")
        ea, eakey = a.f32(128, NT * 64, "tke")
        alpha_all, alakey = a.bf16(128, NT * 64, "alpha_all")
        mcols, mckey = a.f32(128, 3 * NT, "tkm")
        def c3(ap):
            return ap.rearrange("p (t k) -> p t k", k=64)
        def m3(sl):
            return mcols[:, sl * NT:(sl + 1) * NT].rearrange(
                "p (t one) -> p t one", one=1)
        def m3b(sl):
            return m3(sl).broadcast_to([128, NT, 64])
        p.op("vector", lambda v: v.tensor_copy(wk3t, ltok),
             reads=[ltokey], writes=[wkkey], tag="tk cp")
        for it in range(8):
            p.op("vector", lambda v: v.reduce_max(m3(1), c3(wk3t), axis=AX.X),
                 reads=[wkkey], writes=[mckey], tag=f"tk max {it}")
            if it == 0:
                p.op("vector", lambda v: v.tensor_copy(
                    mcols[:, 0:NT], mcols[:, NT:2 * NT]),
                    reads=[mckey], writes=[mckey], tag="tk m1")
            if it < 7:
                p.op("vector", lambda v: v.tensor_tensor(
                    c3(eqa), c3(wk3t), m3b(1), ALU.is_equal),
                    reads=[wkkey, mckey], writes=[eqkey], tag=f"tk eq {it}")
                p.op("vector", lambda v: v.scalar_tensor_tensor(
                    c3(wk3t), c3(eqa), -1e30, c3(wk3t), ALU.mult, ALU.add),
                    reads=[eqkey, wkkey], writes=[wkkey], tag=f"tk msk {it}")
        p.op("vector", lambda v: v.tensor_tensor(
            c3(ea), c3(ltok), m3b(0), ALU.subtract),
            reads=[ltokey, mckey], writes=[eakey], tag="tk sub")
        p.op("vector", lambda v: v.tensor_scalar_mul(ea, ea, tinv),
             reads=[eakey, tinvkey], writes=[eakey], tag="tk sc")
        p.op("scalar", lambda s: s.activation(ea, ea, AF.Exp),
             reads=[eakey], writes=[eakey], tag="tk exp")
        p.op("vector", lambda v: v.tensor_tensor(
            c3(eqa), c3(ltok), m3b(1), ALU.is_ge),
            reads=[ltokey, mckey], writes=[eqkey], tag="tk sel")
        p.op("vector", lambda v: v.tensor_mul(ea, ea, eqa),
             reads=[eakey, eqkey], writes=[eakey], tag="tk es")
        p.op("vector", lambda v: v.reduce_sum(m3(2), c3(ea), axis=AX.X),
             reads=[eakey], writes=[mckey], tag="tk Z")
        p.op("vector", lambda v: v.reciprocal(
            mcols[:, 2 * NT:3 * NT], mcols[:, 2 * NT:3 * NT]),
            reads=[mckey], writes=[mckey], tag="tk Zr")
        p.op("vector", lambda v: v.tensor_tensor(
            c3(alpha_all), c3(ea), m3b(2), ALU.mult),
            reads=[eakey, mckey], writes=[alakey], tag="tk al")
        import os as _os
        if _os.environ.get("DBG_ALPHA"):
            dbg, dbgkey = a.f32(128, NT * 64, "dbgalpha")
            p.op("vector", lambda v: v.tensor_copy(dbg, alpha_all),
                 reads=[alakey], writes=[dbgkey], tag="dbg alpha")
            p.dma("sync", hbm["out"][0:128, :], dbg, reads=[dbgkey],
                  writes=["out@hbm"], tag="dbg alpha st")
        if _os.environ.get("DBG_LOGITS"):
            dbg2, dbg2key = a.f32(128, NT * 64, "dbglog")
            p.op("vector", lambda v: v.tensor_copy(dbg2, ltok),
                 reads=[ltokey], writes=[dbg2key], tag="dbg lg")
            p.dma("sync", hbm["out"][128:256, :], dbg2, reads=[dbg2key],
                  writes=["out@hbm"], tag="dbg lg st")
        for tix in range(NT):
            alphat[tix] = (alpha_all[:, tix * 64:(tix + 1) * 64], alakey)
            bank, bkey = self.banks.take()
            p.op("tensor", lambda t, tix=tix, b=bank: t.transpose(
                b.bitcast(BF16)[0:64, 0:128], alphat[tix][0],
                self.c["identb"][0]),
                reads=[alakey, self.c["identb"][1]], writes=[bkey],
                tag=f"alT {tix}")
            p.op("vector", lambda v, tix=tix, b=bank: v.tensor_copy(
                alphaT[0:64, tix * 128:(tix + 1) * 128],
                b.bitcast(BF16)[0:64, 0:128]),
                reads=[bkey], writes=[alTkey], tag=f"alTev {tix}")
        a.release()

        a.mark()
        bA, bAkey = self.banks.take()
        bB, bBkey = self.banks.take()
        bD, bDkey = self.banks.take()
        for tix in range(NT):
            p.op("tensor", lambda t, tix=tix: t.matmul(
                bA[0:64, 0:512], alphat[tix][0], x3b[tix][0][:, 0:512],
                start=(tix == 0), stop=(tix == NT - 1)),
                reads=[alphat[tix][1], x3b[tix][1]], writes=[bAkey],
                tag=f"siA {tix}")
            p.op("tensor", lambda t, tix=tix: t.matmul(
                bB[0:64, 0:512], alphat[tix][0], x3b[tix][0][:, 512:1024],
                start=(tix == 0), stop=(tix == NT - 1)),
                reads=[alphat[tix][1], x3b[tix][1]], writes=[bBkey],
                tag=f"siB {tix}")
            p.op("tensor", lambda t, tix=tix: t.matmul(
                bD[0:64, 0:1], alphat[tix][0], self.c["ones_col_bf"][0],
                start=(tix == 0), stop=(tix == NT - 1)),
                reads=[alphat[tix][1], self.c["ones_col_bf"][1]],
                writes=[bDkey], tag=f"siD {tix}")
        rd, rdkey = a.f32(64, 2, "rdenom")
        p.op("vector", lambda v: v.tensor_scalar_add(rd[0:64, 0:1],
                                                     bD[0:64, 0:1], 1e-8),
             reads=[bDkey], writes=[rdkey], tag="denom")
        p.op("vector", lambda v: v.reciprocal(rd[0:64, 1:2], rd[0:64, 0:1]),
             reads=[rdkey], writes=[rdkey], tag="rdenom")
        si, sikey = a.f32(64, D, "si")
        p.op("scalar", lambda s: s.activation(si[0:64, 0:512], bA[0:64, 0:512],
                                              AF.Identity,
                                              scale=rd[0:64, 1:2]),
             reads=[bAkey, rdkey], writes=[sikey], tag="siA ev")
        p.op("scalar", lambda s: s.activation(si[0:64, 512:1024],
                                              bB[0:64, 0:512], AF.Identity,
                                              scale=rd[0:64, 1:2]),
             reads=[bBkey, rdkey], writes=[sikey], tag="siB ev")
        sifb, sifbkey = a.bf16(128, ND * 64, "sifb")
        for c in range(ND):
            bank, bkey = self.banks.take()
            p.op("tensor", lambda t, c=c, b=bank: t.transpose(
                b[0:128, 0:64], si[0:64, c * 128:(c + 1) * 128],
                self.c["identf"][0][0:64, 0:64]),
                reads=[sikey, self.c["identf"][1]], writes=[bkey],
                tag=f"siT{c}")
            p.op("vector", lambda v, c=c, b=bank: v.tensor_copy(
                sifb[:, c * 64:(c + 1) * 64], b[0:128, 0:64]),
                reads=[bkey], writes=[sifbkey], tag=f"siTev{c}")

        p.dma("gpsimd", hbm["wihbf"], hbm["L1_ffn_Wih"],
              reads=["L1_ffn_Wih@hbm"], writes=["wihbf@hbm"], tag="wih cast")
        p.dma("gpsimd", hbm["whhbf"], hbm["L1_ffn_Whh"],
              reads=["L1_ffn_Whh@hbm"], writes=["whhbf@hbm"], tag="whh cast")
        bihc, bihckey = self.load_cols("L1_ffn_bih", 3072)
        bhhc, bhhckey = self.load_cols("L1_ffn_bhh", 3072)
        gi, gikey = a.f32(128, 24 * 64, "gi")
        gh, ghkey = a.f32(128, 24 * 64, "gh")
        a.mark()
        wih = []
        whh = []
        for d in range(ND):
            t, key = a.bf16(128, 3072, f"fwih{d}")
            p.dma("sync", t, hbm["wihbf"][:, d * 128:(d + 1) * 128],
                  reads=["wihbf@hbm"], writes=[key], transpose=True,
                  tag=f"fwih{d}")
            wih.append((t, key))
            t, key = a.bf16(128, 3072, f"fwhh{d}")
            p.dma("sync", t, hbm["whhbf"][:, d * 128:(d + 1) * 128],
                  reads=["whhbf@hbm"], writes=[key], transpose=True,
                  tag=f"fwhh{d}")
            whh.append((t, key))
        for (dst, dstkey, w, s_in, sinkey, bc, bckey) in (
                (gi, gikey, wih, sifb, sifbkey, bihc, bihckey),
                (gh, ghkey, whh, Sfb, Sfbkey, bhhc, bhhckey)):
            for oc in range(24):
                bank, bkey = self.banks.take()
                for d in range(ND):
                    p.op("tensor", lambda t, d=d, oc=oc, b=bank, w=w,
                         s_in=s_in: t.matmul(
                             b[0:128, 0:64],
                             w[d][0][:, oc * 128:(oc + 1) * 128],
                             s_in[:, d * 64:(d + 1) * 64],
                             start=(d == 0), stop=(d == ND - 1)),
                         reads=[w[d][1], sinkey], writes=[bkey],
                         tag=f"gru oc{oc} d{d}")
                p.op("scalar", lambda s, oc=oc, b=bank, dst=dst, bc=bc:
                     s.activation(dst[:, oc * 64:(oc + 1) * 64],
                                  b[0:128, 0:64], AF.Identity,
                                  bias=bc[:, oc:oc + 1]),
                     reads=[bkey, bckey], writes=[dstkey],
                     tag=f"gruev oc{oc}")
        a.release()
        Snf, Snfkey = a.f32(128, ND * 64, "Snf")
        Snb, Snbkey = a.bf16(128, ND * 64, "Snb")
        gt, gtkey = a.f32(128, 2 * 64, "gatetmp")
        for c in range(ND):
            sl = slice(c * 64, (c + 1) * 64)
            slz = slice((8 + c) * 64, (9 + c) * 64)
            sln = slice((16 + c) * 64, (17 + c) * 64)
            p.op("vector", lambda v, sl=sl: v.tensor_add(
                gt[:, 0:64], gi[:, sl], gh[:, sl]),
                reads=[gikey, ghkey], writes=[gtkey], tag=f"r0 {c}")
            p.op("scalar", lambda s: s.activation(gt[:, 0:64], gt[:, 0:64],
                                                  AF.Sigmoid),
                 reads=[gtkey], writes=[gtkey], tag=f"r1 {c}")
            p.op("vector", lambda v, sln=sln: v.tensor_mul(
                gt[:, 64:128], gt[:, 0:64], gh[:, sln]),
                reads=[gtkey, ghkey], writes=[gtkey], tag=f"n0 {c}")
            p.op("vector", lambda v, sln=sln: v.tensor_add(
                gt[:, 64:128], gt[:, 64:128], gi[:, sln]),
                reads=[gtkey, gikey], writes=[gtkey], tag=f"n1 {c}")
            p.op("scalar", lambda s: s.activation(gt[:, 64:128],
                                                  gt[:, 64:128], AF.Tanh),
                 reads=[gtkey], writes=[gtkey], tag=f"n2 {c}")
            p.op("vector", lambda v, slz=slz: v.tensor_add(
                gt[:, 0:64], gi[:, slz], gh[:, slz]),
                reads=[gikey, ghkey], writes=[gtkey], tag=f"z0 {c}")
            p.op("scalar", lambda s: s.activation(gt[:, 0:64], gt[:, 0:64],
                                                  AF.Sigmoid),
                 reads=[gtkey], writes=[gtkey], tag=f"z1 {c}")
            p.op("vector", lambda v, sl=sl: v.tensor_sub(
                Snf[:, sl], Sfm[:, sl], gt[:, 64:128]),
                reads=[Sfmkey, gtkey], writes=[Snfkey], tag=f"sn0 {c}")
            p.op("vector", lambda v, sl=sl: v.tensor_mul(
                Snf[:, sl], Snf[:, sl], gt[:, 0:64]),
                reads=[Snfkey, gtkey], writes=[Snfkey], tag=f"sn1 {c}")
            p.op("vector", lambda v, sl=sl: v.tensor_add(
                Snf[:, sl], Snf[:, sl], gt[:, 64:128]),
                reads=[Snfkey, gtkey], writes=[Snfkey], tag=f"sn2 {c}")
            p.op("vector", lambda v, sl=sl: v.tensor_copy(
                Snb[:, sl], Snf[:, sl]),
                reads=[Snfkey], writes=[Snbkey], tag=f"sn3 {c}")

        DQ = 256
        wq = []
        wk = []
        for d in range(ND):
            t, key = a.bf16(128, DQ, f"fwq{d}")
            p.dma("gpsimd", t, hbm["L1_ffn_wq"][d * 128:(d + 1) * 128, :],
                  reads=["L1_ffn_wq@hbm"], writes=[key], tag=f"fwq{d}")
            wq.append((t, key))
            t, key = a.bf16(128, DQ, f"fwk{d}")
            p.dma("gpsimd", t, hbm["L1_ffn_wk"][d * 128:(d + 1) * 128, :],
                  reads=["L1_ffn_wk@hbm"], writes=[key], tag=f"fwk{d}")
            wk.append((t, key))
        bqc, bqckey = self.load_cols("L1_ffn_bq", DQ)
        bkc, bkckey = self.load_cols("L1_ffn_bk", DQ)
        SC = DQ ** (-0.5)
        p.op("vector", lambda v: v.tensor_scalar_mul(bqc, bqc, SC),
             reads=[bqckey], writes=[bqckey], tag="bq sc")
        qf, qfkey = a.bf16(128, 2 * 64, "qf")
        kf, kfkey = a.bf16(128, 2 * 64, "kf")
        for oc in range(2):
            bank, bkey = self.banks.take()
            for d in range(ND):
                p.op("tensor", lambda t, d=d, oc=oc, b=bank: t.matmul(
                    b[0:128, 0:64], wq[d][0][:, oc * 128:(oc + 1) * 128],
                    Snb[:, d * 64:(d + 1) * 64],
                    start=(d == 0), stop=(d == ND - 1)),
                    reads=[wq[d][1], Snbkey], writes=[bkey],
                    tag=f"q oc{oc} d{d}")
            p.op("scalar", lambda s, oc=oc, b=bank: s.activation(
                qf[:, oc * 64:(oc + 1) * 64], b[0:128, 0:64], AF.Identity,
                bias=bqc[:, oc:oc + 1], scale=SC),
                reads=[bkey, bqckey], writes=[qfkey], tag=f"qev oc{oc}")
            bank, bkey = self.banks.take()
            for d in range(ND):
                p.op("tensor", lambda t, d=d, oc=oc, b=bank: t.matmul(
                    b[0:128, 0:64], wk[d][0][:, oc * 128:(oc + 1) * 128],
                    Snb[:, d * 64:(d + 1) * 64],
                    start=(d == 0), stop=(d == ND - 1)),
                    reads=[wk[d][1], Snbkey], writes=[bkey],
                    tag=f"k oc{oc} d{d}")
            p.op("scalar", lambda s, oc=oc, b=bank: s.activation(
                kf[:, oc * 64:(oc + 1) * 64], b[0:128, 0:64], AF.Identity,
                bias=bkc[:, oc:oc + 1]),
                reads=[bkey, bkckey], writes=[kfkey], tag=f"kev oc{oc}")
        wv = []
        for d in range(ND):
            t, key = a.bf16(128, D, f"fwv{d}")
            p.dma("gpsimd", t, hbm["L1_ffn_wv"][d * 128:(d + 1) * 128, :],
                  reads=["L1_ffn_wv@hbm"], writes=[key], tag=f"fwv{d}")
            wv.append((t, key))
        bv_bc, bvkey = self.bcast_row("L1_ffn_bv", D, rows=64)
        vsl, vslkey = a.bf16(128, D, "vslot")
        p.op("vector", lambda v: v.memset(vsl[64:128, :], 0.0),
             writes=[vslkey], tag="v z")
        for ob in range(2):
            bank, bkey = self.banks.take()
            for d in range(ND):
                p.op("tensor", lambda t, d=d, ob=ob, b=bank: t.matmul(
                    b[0:64, 0:512], Snb[:, d * 64:(d + 1) * 64],
                    wv[d][0][:, ob * 512:(ob + 1) * 512],
                    start=(d == 0), stop=(d == ND - 1)),
                    reads=[Snbkey, wv[d][1]], writes=[bkey],
                    tag=f"v ob{ob} d{d}")
            p.op("vector", lambda v, ob=ob, b=bank: v.tensor_tensor(
                vsl[0:64, ob * 512:(ob + 1) * 512], b[0:64, 0:512],
                bv_bc[0:64, ob * 512:(ob + 1) * 512], ALU.add),
                reads=[bkey, bvkey], writes=[vslkey], tag=f"vev ob{ob}")

        sim, simkey = a.f32(64, 80, "sim")
        bank, bkey = self.banks.take()
        for oc in range(2):
            p.op("tensor", lambda t, oc=oc, b=bank: t.matmul(
                b[0:64, 0:64], qf[:, oc * 64:(oc + 1) * 64],
                kf[:, oc * 64:(oc + 1) * 64],
                start=(oc == 0), stop=(oc == 1)),
                reads=[qfkey, kfkey], writes=[bkey], tag=f"sim oc{oc}")
        p.op("vector", lambda v, b=bank: v.tensor_copy(sim[0:64, 0:64],
                                                       b[0:64, 0:64]),
             reads=[bkey], writes=[simkey], tag="sim ev")
        p.op("vector", lambda v: v.reduce_max(sim[0:64, 64:65],
                                              sim[0:64, 0:64], axis=AX.X),
             reads=[simkey], writes=[simkey], tag="sim max")
        p.op("vector", lambda v: v.tensor_scalar_mul(sim[0:64, 64:65],
                                                     sim[0:64, 64:65], -1.0),
             reads=[simkey], writes=[simkey], tag="sim neg")
        p.op("scalar", lambda s: s.activation(sim[0:64, 0:64],
                                              sim[0:64, 0:64], AF.Exp,
                                              bias=sim[0:64, 64:65]),
             reads=[simkey], writes=[simkey], tag="sim exp")
        p.op("vector", lambda v: v.tensor_scalar(
            sim[0:64, 66:67], sim[0:64, 0:64], 1.0, 0.0, ALU.mult, ALU.add,
            accum_out=sim[0:64, 65:66]) if False else v.reduce_sum(
            sim[0:64, 65:66], sim[0:64, 0:64], axis=AX.X),
            reads=[simkey], writes=[simkey], tag="sim Z")
        p.op("vector", lambda v: v.reciprocal(sim[0:64, 65:66],
                                              sim[0:64, 65:66]),
             reads=[simkey], writes=[simkey], tag="sim Zr")
        p.op("vector", lambda v: v.tensor_scalar_mul(
            sim[0:64, 0:64], sim[0:64, 0:64], sim[0:64, 65:66]),
            reads=[simkey], writes=[simkey], tag="sim n")
        Pb, Pbkey = a.bf16(64, 64, "Pb")
        p.op("vector", lambda v: v.tensor_copy(Pb, sim[0:64, 0:64]),
             reads=[simkey], writes=[Pbkey], tag="P cast")
        PT, PTkey = a.bf16(128, 64, "PT")
        p.op("vector", lambda v: v.memset(PT[64:128, :], 0.0),
             writes=[PTkey], tag="PT z")
        bank, bkey = self.banks.take()
        p.op("tensor", lambda t, b=bank: t.transpose(
            b.bitcast(BF16)[0:64, 0:64], Pb, self.c["identb"][0][0:64, 0:64]),
            reads=[Pbkey, self.c["identb"][1]], writes=[bkey], tag="PT T")
        p.op("vector", lambda v, b=bank: v.tensor_copy(
            PT[0:64, :], b.bitcast(BF16)[0:64, 0:64]),
            reads=[bkey], writes=[PTkey], tag="PT ev")

        Snsl, Snslkey = a.f32(64, D, "Snslot")
        for c in range(ND):
            bank, bkey = self.banks.take()
            p.op("tensor", lambda t, c=c, b=bank: t.transpose(
                b[0:64, 0:128], Snf[:, c * 64:(c + 1) * 64],
                self.c["identf"][0]),
                reads=[Snfkey, self.c["identf"][1]], writes=[bkey],
                tag=f"SnT{c}")
            p.op("vector", lambda v, c=c, b=bank: v.tensor_copy(
                Snsl[0:64, c * 128:(c + 1) * 128], b[0:64, 0:128]),
                reads=[bkey], writes=[Snslkey], tag=f"SnTev{c}")
        y, ykey = a.f32(64, D, "y")
        for ob in range(2):
            bank, bkey = self.banks.take()
            p.op("tensor", lambda t, ob=ob, b=bank: t.matmul(
                b[0:64, 0:512], PT, vsl[:, ob * 512:(ob + 1) * 512],
                start=True, stop=True),
                reads=[PTkey, vslkey], writes=[bkey], tag=f"scm ob{ob}")
            p.op("vector", lambda v, ob=ob, b=bank: v.tensor_tensor(
                y[0:64, ob * 512:(ob + 1) * 512], b[0:64, 0:512],
                Snsl[0:64, ob * 512:(ob + 1) * 512], ALU.add),
                reads=[bkey, Snslkey], writes=[ykey], tag=f"y ob{ob}")
        yn, ynkey = a.f32(64, D, "yn")
        self.ln_tile(y, ykey, yn, ynkey, rows=64, tag="l1f")
        lng_bc, lngkey = self.bcast_row("L1_ffn_ln_g", D, rows=64)
        lnb_bc, lnbkey = self.bcast_row("L1_ffn_ln_b", D, rows=64)
        p.op("vector", lambda v: v.tensor_mul(yn, yn, lng_bc),
             reads=[ynkey, lngkey], writes=[ynkey], tag="yg")
        p.op("vector", lambda v: v.tensor_add(yn, yn, lnb_bc),
             reads=[ynkey, lnbkey], writes=[ynkey], tag="yb")
        ynfb, ynfbkey = a.bf16(128, ND * 64, "ynfb")
        for c in range(ND):
            bank, bkey = self.banks.take()
            p.op("tensor", lambda t, c=c, b=bank: t.transpose(
                b[0:128, 0:64], yn[0:64, c * 128:(c + 1) * 128],
                self.c["identf"][0][0:64, 0:64]),
                reads=[ynkey, self.c["identf"][1]], writes=[bkey],
                tag=f"ynT{c}")
            p.op("vector", lambda v, c=c, b=bank: v.tensor_copy(
                ynfb[:, c * 64:(c + 1) * 64], b[0:128, 0:64]),
                reads=[bkey], writes=[ynfbkey], tag=f"ynTev{c}")

        wvp = []
        for d in range(ND):
            t, key = a.bf16(128, D, f"fwvp{d}")
            p.dma("gpsimd", t, hbm["L1_ffn_wvp"][d * 128:(d + 1) * 128, :],
                  reads=["L1_ffn_wvp@hbm"], writes=[key], tag=f"fwvp{d}")
            wvp.append((t, key))
        bvpc, bvpckey = self.load_cols("L1_ffn_bvp", D)
        svf, svfkey = a.f32(128, 64, "svf")
        svb, svbkey = a.bf16(128, 64, "svb")
        svsl, svslkey = a.bf16(128, D, "svslot")
        p.op("vector", lambda v: v.memset(svsl[64:128, :], 0.0),
             writes=[svslkey], tag="sv z")
        for oc in range(ND):
            bank, bkey = self.banks.take()
            for d in range(ND):
                p.op("tensor", lambda t, d=d, oc=oc, b=bank: t.matmul(
                    b[0:128, 0:64], wvp[d][0][:, oc * 128:(oc + 1) * 128],
                    ynfb[:, d * 64:(d + 1) * 64],
                    start=(d == 0), stop=(d == ND - 1)),
                    reads=[wvp[d][1], ynfbkey], writes=[bkey],
                    tag=f"sv oc{oc} d{d}")
            p.op("scalar", lambda s, oc=oc, b=bank: s.activation(
                svf, b[0:128, 0:64], AF.Identity, bias=bvpc[:, oc:oc + 1]),
                reads=[bkey, bvpckey], writes=[svfkey], tag=f"svev oc{oc}")
            p.op("vector", lambda v: v.tensor_copy(svb, svf),
                 reads=[svfkey], writes=[svbkey], tag=f"svc oc{oc}")
            bank, bkey = self.banks.take()
            p.op("tensor", lambda t, oc=oc, b=bank: t.transpose(
                b.bitcast(BF16)[0:64, 0:128], svb, self.c["identb"][0]),
                reads=[svbkey, self.c["identb"][1]], writes=[bkey],
                tag=f"svT oc{oc}")
            p.op("vector", lambda v, oc=oc, b=bank: v.tensor_copy(
                svsl[0:64, oc * 128:(oc + 1) * 128],
                b.bitcast(BF16)[0:64, 0:128]),
                reads=[bkey], writes=[svslkey], tag=f"svTev oc{oc}")
        a.release()

        a.mark()
        OF = [a.bf16(128, T, f"fOF{c}") for c in range(ND)]
        for dc in range(ND):
            for tb in range(4):
                bank, bkey = self.banks.take()
                p.op("tensor", lambda t, dc=dc, tb=tb, b=bank: t.matmul(
                    b, svsl[:, dc * 128:(dc + 1) * 128],
                    alphaT[:, tb * 512:(tb + 1) * 512],
                    start=True, stop=True),
                    reads=[svslkey, alTkey], writes=[bkey],
                    tag=f"OF dc{dc} tb{tb}")
                p.op("vector", lambda v, dc=dc, tb=tb, b=bank: v.tensor_copy(
                    OF[dc][0][:, tb * 512:(tb + 1) * 512], b),
                    reads=[bkey], writes=[OF[dc][1]], tag=f"OFev dc{dc} tb{tb}")
        wo = []
        for d in range(ND):
            t, key = a.bf16(128, D, f"fwo{d}")
            p.dma("gpsimd", t, hbm["L1_ffn_wo"][d * 128:(d + 1) * 128, :],
                  reads=["L1_ffn_wo@hbm"], writes=[key], tag=f"fwo{d}")
            wo.append((t, key))
        bo_bc, bokey = self.bcast_row("L1_ffn_bo", D)
        x3t = [a.f32(128, D, f"cx3{k}") for k in range(2)]
        xpre = [a.f32(128, D, f"cxp{k}") for k in range(2)]
        x4t = [a.f32(128, D, f"cx4{k}") for k in range(2)]
        for tc in range(NT):
            x3, x3key = x3t[tc % 2]
            xp, xpkey = xpre[tc % 2]
            x4, x4key = x4t[tc % 2]
            p.dma("sync", x3, hbm["x3"][tc * 128:(tc + 1) * 128, :],
                  reads=["x3@hbm"], writes=[x3key], tag=f"cx3 ld {tc}")
            for ob in range(2):
                bank, bkey = self.banks.take()
                for dc in range(ND):
                    p.op("tensor", lambda t, dc=dc, tc=tc, ob=ob, b=bank:
                         t.matmul(b, OF[dc][0][:, tc * 128:(tc + 1) * 128],
                                  wo[dc][0][:, ob * 512:(ob + 1) * 512],
                                  start=(dc == 0), stop=(dc == ND - 1)),
                         reads=[OF[dc][1], wo[dc][1]], writes=[bkey],
                         tag=f"fwo tc{tc} ob{ob} dc{dc}")
                p.op("vector", lambda v, b=bank, xp=xp, ob=ob: v.tensor_tensor(
                    xp[:, ob * 512:(ob + 1) * 512], b,
                    bo_bc[:, ob * 512:(ob + 1) * 512], ALU.add),
                    reads=[bkey, bokey], writes=[xpkey],
                    tag=f"fwoev tc{tc} ob{ob}")
            p.op("vector", lambda v, xp=xp, x3=x3: v.tensor_add(xp, xp, x3),
                 reads=[xpkey, x3key], writes=[xpkey], tag=f"cres {tc}")
            self.ln_tile(xp, xpkey, x4, x4key, tag=f"l1f{tc}")
            import os as _os2
            _dst = hbm["x1"] if _os2.environ.get("DBG_ALPHA") else hbm["out"]
            p.dma("sync", _dst[tc * 128:(tc + 1) * 128, :], x4,
                  reads=[x4key], writes=["outfinal@hbm"], tag=f"out st {tc}")
        a.release()
        a.release()



def build_program(upto=4):
    nc = bass.Bass("TRN2", target_bir_lowering=False)
    hbm = {}
    shapes = {
        "L0_attn_x": ([T, D], F32),
        "L0_attn_w_qkv": ([D, 3 * D], F32),
        "L0_attn_b_qkv": ([3 * D], F32),
        "L0_attn_w_proj": ([D, D], F32),
        "L0_attn_b_proj": ([D], F32),
        "L0_ffn_w1_w": ([4096, D], F32),
        "L0_ffn_w1_s": ([4096], F32),
        "L0_ffn_w2_w": ([4096, D], F32),
        "L0_ffn_w2_s": ([4096], F32),
        "L0_ffn_h1_w": ([4096, 4096], F32),
        "L0_ffn_h1_s": ([4096], F32),
        "L0_ffn_h2_w": ([4096, 4096], F32),
        "L0_ffn_h2_s": ([4096], F32),
        "L0_ffn_w3_w": ([D, 4096], F32),
        "L0_ffn_w3_s": ([D], F32),
        "L1_attn_ln_scale": ([D], F32),
        "L1_attn_ln_bias": ([D], F32),
        "L1_attn_W_hidden": ([D, 4096], F32),
        "L1_attn_b_hidden": ([4096], F32),
        "L1_attn_W_qk": ([D, 128], F32),
        "L1_attn_b_qk": ([128], F32),
        "L1_attn_os_weight": ([2, 128], F32),
        "L1_attn_os_bias": ([2, 128], F32),
        "L1_attn_W_out": ([2 * D, D], F32),
        "L1_attn_b_out": ([D], F32),
        "L1_ffn_slot_init": ([64, D], F32),
        "L1_ffn_w1": ([2 * D, 512], F32),
        "L1_ffn_b1": ([512], F32),
        "L1_ffn_w2": ([512, 64], F32),
        "L1_ffn_b2": ([64], F32),
        "L1_ffn_tau": ([1], F32),
        "L1_ffn_Wih": ([3 * D, D], F32),
        "L1_ffn_Whh": ([3 * D, D], F32),
        "L1_ffn_bih": ([3 * D], F32),
        "L1_ffn_bhh": ([3 * D], F32),
        "L1_ffn_wq": ([D, 256], F32),
        "L1_ffn_bq": ([256], F32),
        "L1_ffn_wk": ([D, 256], F32),
        "L1_ffn_bk": ([256], F32),
        "L1_ffn_wv": ([D, D], F32),
        "L1_ffn_bv": ([D], F32),
        "L1_ffn_ln_g": ([D], F32),
        "L1_ffn_ln_b": ([D], F32),
        "L1_ffn_wvp": ([D, D], F32),
        "L1_ffn_bvp": ([D], F32),
        "L1_ffn_wo": ([D, D], F32),
        "L1_ffn_bo": ([D], F32),
    }
    for name, (shape, dt) in shapes.items():
        hbm[name] = nc.dram_tensor(name, shape, dt, kind="ExternalInput").ap()
    hbm["out"] = nc.dram_tensor("out", [T, D], F32, kind="ExternalOutput").ap()

    scratch = {
        "x0bf": ([T, D], BF16), "x1": ([T, D], F32), "x1bf": ([T, D], BF16),
        "x2": ([T, D], F32), "x2bf": ([T, D], BF16),
        "x3": ([T, D], F32), "x3bf": ([T, D], BF16),
        "vfm": ([D, T], BF16), "rrow": ([T], F32),
        "w1bf": ([4096, D], BF16), "w2bf": ([4096, D], BF16),
        "w3bf": ([D, 4096], BF16),
        "h1bf": ([4096, 4096], BF16), "h2bf": ([4096, 4096], BF16),
        "h": ([T, 4096], BF16), "g1": ([T, 4096], BF16),
        "g2": ([T, 4096], BF16),
        "xnbf": ([T, D], BF16), "gate": ([T, 2048], BF16),
        "tauw": ([1], F32), "wihbf": ([3072, D], BF16),
        "whhbf": ([3072, D], BF16),
    }
    for name, (shape, dt) in scratch.items():
        hbm[name] = nc.dram_tensor("scr_" + name, shape, dt,
                                   kind="Internal").ap()

    words = 53000
    arena_t = nc.alloc_sbuf_tensor("arena", [128, words], F32)
    arena = Arena(arena_t.ap(), words)

    bank_aps = []
    bank_handles = []
    for i in range(8):
        h = nc.alloc_psum_tensor(f"bank{i}", [128, 512], F32)
        bank_handles.append(h)
        bank_aps.append((h.ap(), f"psum{i}"))
    banks = Banks(bank_aps)

    prog = Prog(nc)
    b = Builder(nc, prog, arena, banks, hbm)
    b.prelude()
    if upto >= 1:
        b.l0_attn()
    if upto >= 2:
        b.l0_ffn()
    if upto >= 3:
        b.l1_attn()
    if upto >= 4:
        b.l1_ffn()
    final = {1: "x1", 2: "x2", 3: "x3", 4: "x3"}[upto]
    if upto < 4:
        prog.dma("sync", hbm["out"].rearrange("a b -> (a b)")[None, :],
                 hbm[final].rearrange("a b -> (a b)")[None, :],
                 reads=[final + "@hbm"], writes=["out@hbm"], tag="final cp")

    comp_sems = {}
    dma_sems = {}
    import contextlib
    stack = contextlib.ExitStack()
    for e in ENGINES:
        comp_sems[e] = stack.enter_context(nc.semaphore(f"c_{e}"))
        dma_sems[e] = stack.enter_context(nc.semaphore(f"d_{e}"))
    n, ms, dc = prog.emit(comp_sems, dma_sems)
    stack.close()
    return nc, n, ms, dc, arena.peak


def kernel(**inputs):
    x = np.ascontiguousarray(inputs["L0_attn_x"])
    nc, n, ms, dc, peak = build_program(upto=4)
    weights = {k: np.ascontiguousarray(v) for k, v in inputs.items()
               if k != "L0_attn_x"}
    in_maps = []
    for core in range(8):
        m = dict(weights)
        m["L0_attn_x"] = x[core]
        in_maps.append(m)
    res = run_bass_kernel_spmd(nc, in_maps, core_ids=list(range(8)))
    return np.stack([r["out"] for r in res.results], axis=0)


if __name__ == "__main__":
    nc, n, ms, dc, peak = build_program(upto=1)
    print("instructions:", n, "milestones:", ms, "dmas:", dc,
          "arena peak words:", peak)
